# version 24
# speedup vs baseline: 1.4108x; 1.1075x over previous
import contextlib
import numpy as np
import ml_dtypes
import concourse.bass as bass
import concourse.mybir as mybir
from concourse.bass_utils import run_bass_kernel_spmd

F32 = mybir.dt.float32
BF16 = mybir.dt.bfloat16
AF = mybir.ActivationFunctionType
ALU = mybir.AluOpType
AXX = mybir.AxisListType.X

S = 2048
D = 1024
NT = D // 128
HD = 64
H16 = 16
H8 = 8
HID = 2730
HIDP = 2816
NTH = HIDP // 128
WIN = 512
LAM = 0.01
BF = ml_dtypes.bfloat16
SQ2I = 0.7071067811865476



def _tile_lhsT(w):
    din, dout = w.shape
    nc_, nt_ = din // 128, dout // 128
    return np.ascontiguousarray(w.reshape(nc_, 128, nt_, 128).transpose(2, 0, 1, 3))


def _host_consts():
    c = {}
    p = np.arange(128)[:, None]
    j = np.arange(4096)[None, :]
    c["decay"] = (np.exp(-LAM * np.abs(p - j + 2048)) / 32.0).astype(np.float32)
    cc = np.arange(2048)[None, :] * np.ones((128, 1))
    c["edec"] = np.stack([np.exp(LAM * cc), np.exp(-LAM * cc) / 32.0]
                         ).astype(np.float32)
    c["ident"] = np.eye(128, dtype=BF)
    c["ones_bf"] = np.ones((128, 1), dtype=BF)
    half = np.arange(0, HD, 2, dtype=np.float64)
    inv_freq = 1.0 / 10000.0 ** (half / HD)
    t = np.arange(S, dtype=np.float64)
    freqs = np.concatenate([t[:, None] * inv_freq[None, :]] * 2, axis=1)
    base = (half + 0.4 * HD) / (1.4 * HD)
    power = (t - S // 2) / float(WIN // 2)
    scale = np.concatenate([base[None, :] ** power[:, None]] * 2, axis=1)
    cos, sin = np.cos(freqs), np.sin(freqs)
    sgn = np.ones((S, HD))
    sgn[:, 0:HD // 2] = -1.0
    for name, m in [("rot_qc1", 8.0 * cos * scale), ("rot_qc2", 8.0 * sin * scale * sgn),
                    ("rot_kc3", cos / scale), ("rot_kc4", sin / scale * sgn)]:
        mt = m.T
        c[name] = np.ascontiguousarray(
            np.concatenate([mt, mt], axis=0).astype(np.float32))
    masks = np.zeros((5, 128, 640), np.float32)
    cc = np.arange(640)[None, :]
    pp = np.arange(128)[:, None]
    NEG = np.float32(-3.0e38)
    for qt in range(4):
        masks[qt] = np.where(cc <= (qt * 128 + pp), 0.0, NEG)
    masks[4] = np.where((cc >= pp) & (cc <= pp + WIN), 0.0, NEG)
    c["lmask"] = masks
    return c


def _host_prep(inputs):
    sh = _host_consts()
    sh["wq"] = _tile_lhsT(np.asarray(inputs["L0_attn_W_q"]))
    sh["wk"] = _tile_lhsT(np.asarray(inputs["L0_attn_W_k"]))
    sh["wv"] = _tile_lhsT(np.asarray(inputs["L0_attn_W_v"]))
    sh["wdown"] = _tile_lhsT(np.asarray(inputs["L0_attn_W_down_kv"]))
    sh["wupk"] = _tile_lhsT(np.asarray(inputs["L0_attn_W_up_k"]))
    sh["wupv"] = np.ascontiguousarray(
        np.asarray(inputs["L0_attn_W_up_v"]).reshape(2, 128, D))
    sh["wo"] = _tile_lhsT(np.asarray(inputs["L0_attn_W_o"]))

    wqkv = np.asarray(inputs["L0_ffn_wqkv"])
    sh["fwqk"] = np.stack([_tile_lhsT(wqkv[l][:, 0:1024]) for l in range(2)])
    sh["fwv"] = np.ascontiguousarray(wqkv[:, :, 1024:1536].reshape(2, NT, 128, 512))
    fwo = np.asarray(inputs["L0_ffn_wo"])
    sh["fwo"] = np.stack([_tile_lhsT(fwo[l]) for l in range(2)])
    w1 = np.asarray(inputs["L0_ffn_w1"])
    w1a = np.zeros((2, D, HIDP), np.float32)
    w1g = np.zeros((2, D, HIDP), np.float32)
    w1a[:, :, 0:HID] = w1[:, :, 0:HID]
    w1g[:, :, 0:HID] = w1[:, :, HID:2 * HID]
    sh["fw1a"] = np.stack([_tile_lhsT(w1a[l]) for l in range(2)])
    sh["fw1g"] = np.stack([_tile_lhsT(w1g[l]) for l in range(2)])
    w2 = np.asarray(inputs["L0_ffn_w2"])
    w2p = np.zeros((2, HIDP, D), np.float32)
    w2p[:, 0:HID, :] = w2
    sh["fw2"] = np.stack([_tile_lhsT(w2p[l]) for l in range(2)])
    b1 = np.asarray(inputs["L0_ffn_b1"])
    b1a = np.zeros((2, HIDP), np.float32)
    b1g = np.zeros((2, HIDP), np.float32)
    b1a[:, 0:HID] = b1[:, 0:HID]
    b1g[:, 0:HID] = b1[:, HID:2 * HID]
    sh["fb1a"] = np.ascontiguousarray(b1a.reshape(2, NTH, 128).transpose(0, 2, 1))
    sh["fb1g"] = np.ascontiguousarray(b1g.reshape(2, NTH, 128).transpose(0, 2, 1))
    sh["fb2"] = np.ascontiguousarray(
        np.asarray(inputs["L0_ffn_b2"]).reshape(2, NT, 128).transpose(0, 2, 1))
    sh["flng"] = np.ascontiguousarray(
        np.asarray(inputs["L0_ffn_ln_g"]).reshape(2, NT, 128).transpose(0, 2, 1))
    sh["flnb"] = np.ascontiguousarray(
        np.asarray(inputs["L0_ffn_ln_b"]).reshape(2, NT, 128).transpose(0, 2, 1))
    sh["fqs"] = np.ascontiguousarray(
        np.tile(np.asarray(inputs["L0_ffn_q_scale"]), (1, 2))[:, :, None])
    sh["fks"] = np.ascontiguousarray(
        np.tile(np.asarray(inputs["L0_ffn_k_scale"]), (1, 2))[:, :, None])
    x = np.asarray(inputs["L0_attn_x"])
    xts = [np.ascontiguousarray(x[i].T) for i in range(x.shape[0])]
    return sh, xts



ENGS = ("Y", "G", "T", "V", "S")


class Prog:

    NPOOL = 10

    def __init__(self, sems, pools):
        self.sems = sems
        self.pools = pools
        self.q = {e: [] for e in ENGS}
        self.cnt = {k: 0 for k in sems}
        self.pool_i = {e: 0 for e in ("Y", "G")}
        self.lastw = {}

    def _wait(self, eng, deps):
        w = []
        for d in deps:
            if d is None:
                continue
            key, val = d
            k = (eng, key)
            if self.lastw.get(k, 0) >= val:
                continue
            self.lastw[k] = val
            w.append((key, val))
        return w

    def op(self, eng, fn, deps=(), inc=1, tag=""):
        w = self._wait(eng, deps)
        self.q[eng].append((w, fn, eng if inc else None, inc, tag))
        if inc:
            self.cnt[eng] += inc
        return (eng, self.cnt[eng])

    def dma(self, eng, out_ap, in_ap, deps=(), tag=""):
        key = self.pools[eng][self.pool_i[eng] % self.NPOOL]
        self.pool_i[eng] += 1
        gate = (key, self.cnt[key]) if self.cnt[key] else None
        w = self._wait(eng, list(deps) + ([gate] if gate else []))
        fn = lambda e, o=out_ap, i=in_ap: e.dma_start(o, i)
        self.q[eng].append((w, fn, key, 16, tag))
        self.cnt[key] += 16
        return (key, self.cnt[key])

    def tok(self, eng):
        return (eng, self.cnt[eng])

    def dtok(self, eng):
        return [(k, self.cnt[k]) for k in self.pools[eng] if self.cnt[k]]

    def join(self, engs=ENGS):
        toks = [(e, self.cnt[e]) for e in ("T", "V", "S")] +             self.dtok("Y") + self.dtok("G")
        out = []
        for e in engs:
            if e == "T":
                out.append(self.op("T", lambda t: t.nop(), deps=toks, inc=1,
                                   tag="joinT"))
            elif e == "V":
                out.append(self.op("V", lambda v: v.engine_nop(), deps=toks,
                                   inc=1, tag="joinV"))
            elif e == "S":
                out.append(self.op("S", lambda s: s.nop(), deps=toks,
                                   inc=1, tag="joinS"))
            elif e in ("Y", "G"):
                out.append(self.op(e, lambda q: q.nop(), deps=toks, inc=1,
                                   tag="join" + e))
        return out

    def emit(self, blk):
        sems = self.sems

        def body(engname):
            def _(eng):
                for w, fn, skey, inc, tag in self.q[engname]:
                    for key, val in w:
                        eng.wait_ge(sems[key], val)
                    inst = fn(eng)
                    if inc:
                        assert inst is not None, f"inc op returned None: {tag}"
                        inst.then_inc(sems[skey], inc)
            return _
        blk.sync(body("Y"))
        blk.gpsimd(body("G"))
        blk.tensor(body("T"))
        blk.vector(body("V"))
        blk.scalar(body("S"))



def build(upto="full", dump=(), joins=()):
    nc = bass.Bass("TRN2", target_bir_lowering=False)
    dump = set(dump)

    h = {}
    h["xT"] = nc.dram_tensor("xT", [D, S], F32, kind="ExternalInput").ap()
    shapes = dict(
        wq=[NT, NT, 128, 128], wk=[NT, NT, 128, 128], wv=[NT, NT, 128, 128],
        wdown=[2, NT, 128, 128], wupk=[NT, 2, 128, 128], wupv=[2, 128, D],
        wo=[NT, NT, 128, 128],
        fwqk=[2, NT, NT, 128, 128], fwv=[2, NT, 128, 512],
        fwo=[2, NT, 4, 128, 128],
        fw1a=[2, NTH, NT, 128, 128], fw1g=[2, NTH, NT, 128, 128],
        fw2=[2, NT, NTH, 128, 128],
        fb1a=[2, 128, NTH], fb1g=[2, 128, NTH], fb2=[2, 128, NT],
        flng=[2, 128, NT], flnb=[2, 128, NT], fqs=[2, 128, 1], fks=[2, 128, 1],
        decay=[128, 4096], edec=[2, 128, 2048],
        rot_qc1=[128, S], rot_qc2=[128, S], rot_kc3=[128, S], rot_kc4=[128, S],
        lmask=[5, 128, 640],
    )
    for k, shp in shapes.items():
        h[k] = nc.dram_tensor(k, shp, F32, kind="ExternalInput").ap()
    h["ident"] = nc.dram_tensor("ident", [128, 128], BF16, kind="ExternalInput").ap()
    h["ones_bf"] = nc.dram_tensor("ones_bf", [128, 1], BF16, kind="ExternalInput").ap()
    h["outT"] = nc.dram_tensor("outT", [D, S], F32, kind="ExternalOutput").ap()
    h["x1_hbm"] = nc.dram_tensor("x1_hbm", [NT, 128, S], F32, kind="Internal").ap()
    h["res_hbm"] = nc.dram_tensor("res_hbm", [NT, 128, S], F32, kind="Internal").ap()
    h["qT_sp"] = nc.dram_tensor("qT_sp", [NT, 128, S], BF16, kind="Internal").ap()
    h["kr_sp"] = nc.dram_tensor("kr_sp", [NT, 128, S], BF16, kind="Internal").ap()
    h["rows_hbm"] = nc.dram_tensor("rows_hbm", [8, S], F32, kind="Internal").ap()
    dumps = {name: nc.dram_tensor("dbg_" + name, [D, S], F32,
                                  kind="ExternalOutput").ap() for name in dump}

    st = contextlib.ExitStack()
    with st:
        def sb(name, cols, dtype=F32):
            tns = st.enter_context(nc.sbuf_tensor(name, [128, cols], dtype))
            return tns[:]

        resb = sb("resb", 16384, BF16)
        A1 = sb("A1", 8320)
        A2 = sb("A2", 8192)
        A3 = sb("A3", 4096)
        A4 = sb("A4", 4096)
        A5 = sb("A5", 4096)
        A6 = sb("A6", 4096)
        WS = sb("WS", 3072)
        RW = sb("RW", 2048)
        BC = sb("BC", 2048)
        CST = sb("CST", 512)

        psA = st.enter_context(nc.psum_tensor("psA", [128, 2048], F32))[:]
        psB = st.enter_context(nc.psum_tensor("psB", [128, 2048], F32))[:]

        sems = {}
        pools = {}
        for e in ("T", "V", "S", "Y", "G"):
            sems[e] = st.enter_context(nc.semaphore("sem" + e))
        for e in ("Y", "G"):
            pools[e] = []
            for i in range(Prog.NPOOL):
                key = e + str(i)
                sems[key] = st.enter_context(nc.semaphore("sem" + key))
                pools[e].append(key)
        blk = st.enter_context(nc.Block())
        P = Prog(sems, pools)

        ident_bf = CST[:, 0:64].bitcast(BF16)
        ones_bf = CST[:, 64:65].bitcast(BF16)
        qs_col = CST[:, 66:67]
        ks_col = CST[:, 67:68]
        b1a_cols = CST[:, 96:96 + NTH]
        b1g_cols = CST[:, 128:128 + NTH]
        b1gs_cols = CST[:, 160:160 + NTH]
        b2_cols = CST[:, 192:192 + NT]
        lng_cols = CST[:, 208:208 + NT]
        lnb_cols = CST[:, 224:224 + NT]
        eps_col = CST[:, 240:241]
        nm_cols = [CST[:, 244 + i:245 + i] for i in range(2)]
        rc_cols = [CST[:, 248 + i:249 + i] for i in range(2)]
        rowA = RW[:, 0:2048]

        c_tok = [P.dma("Y", ident_bf[:, 0:128], h["ident"][:], tag="ident"),
                 P.dma("Y", ones_bf[:, 0:1], h["ones_bf"][:], tag="ones"),
                 P.op("V", lambda v: v.memset(eps_col[0:1, 0:1], 1e-5), tag="eps")]
        resb_t = [resb[:, t * S:(t + 1) * S] for t in range(NT)]

        ws_state = [(), ()]

        def wstream(hbm_blocks, n_c, slot, deps=()):
            stage = WS[:, slot * 1536: slot * 1536 + n_c * 128]
            wbuf = WS[:, slot * 1536 + 1024: slot * 1536 + 1024 + n_c * 64].bitcast(BF16)
            tok = P.dma("Y", stage.rearrange("p (c f) -> p c f", f=128),
                        hbm_blocks.rearrange("c p f -> p c f"),
                        deps=list(deps) + list(ws_state[slot]), tag="wload")
            ctok = P.op("V", lambda v, o=wbuf, i=stage: v.tensor_copy(o, i),
                        deps=[tok], tag="wcast")
            return wbuf, ctok

        def mm_group(ps_ap, lhs_fn, rhs_fn, n_c, deps, tag, n_nc=4, ncw=512,
                     start_c=0, stop_c=None, tp=None):
            stop_c = n_c - 1 if stop_c is None else stop_c
            tok = None
            for ncx in range(n_nc):
                for ci in range(n_c):
                    def mm(T, ci=ci, ncx=ncx):
                        return T.matmul(ps_ap[:, ncx * ncw:(ncx + 1) * ncw],
                                        lhs_fn(ci), rhs_fn(ci, ncx),
                                        start=(ci == start_c), stop=(ci == stop_c),
                                        tile_position=tp)
                    tok = P.op("T", mm, deps=deps if (ncx == 0 and ci == 0) else (),
                               inc=(1 if (ncx == n_nc - 1 and ci == n_c - 1) else 0),
                               tag=tag)
            return tok

        bcast_slot_cons = {}

        def bcast(src_row_ap, dst_ap, nrows, hbm_slot, deps, tag="bc"):
            r = h["rows_hbm"][hbm_slot:hbm_slot + 1, :]
            t1 = P.dma("G", r, src_row_ap,
                       deps=list(deps) + [bcast_slot_cons.get(hbm_slot)],
                       tag=tag + "o")
            t2 = P.dma("G", dst_ap, r.broadcast_to([nrows, S]), deps=[t1],
                       tag=tag + "i")
            bcast_slot_cons[hbm_slot] = t2
            return t2

        def dump_tile(name, t, ap, deps):
            if name in dumps:
                P.dma("G", dumps[name][t * 128:(t + 1) * 128, :], ap, deps=deps,
                      tag="dump")

        def ln_stats(bf_tiles, sq_tiles, deps):
            last = None
            for which, rhs in ((0, bf_tiles), (32, sq_tiles)):
                for ncx in range(4):
                    for c in range(NT):
                        def mm(T, which=which, ncx=ncx, c=c, rhs=rhs):
                            return T.matmul(
                                psB[which:which + 1, ncx * 512:(ncx + 1) * 512],
                                ones_bf[:, 0:1],
                                rhs[c][:, ncx * 512:(ncx + 1) * 512],
                                start=(c == 0), stop=(c == NT - 1),
                                tile_position=(0, which))
                        last = P.op("T", mm,
                                    deps=deps if (which == 0 and ncx == 0 and c == 0) else (),
                                    inc=(1 if (c == NT - 1) else 0), tag="lnstat")
            return last

        def ln_rows(stok, tag):
            R0 = rowA[0:1, :]
            R1 = BC[0:1, :]
            R2 = A5[0:1, 0:2048]
            a1 = P.op("V", lambda v: v.tensor_scalar_mul(R0, psB[0:1, :], 1.0 / D),
                      deps=[stok], tag=tag + "_mu")
            c0 = P.op("V", lambda v: v.tensor_copy(rowA[32:33, :], psB[32:33, :]),
                      deps=[stok], tag=tag + "_s2ev")
            d0 = P.dma("Y", R1, rowA[32:33, :], deps=[c0], tag=tag + "_s2sh")
            a2 = P.op("V", lambda v: v.tensor_scalar_mul(R1, R1, 1.0 / D),
                      deps=[d0], tag=tag + "_m2")
            a3 = P.op("V", lambda v: v.tensor_mul(R2, R0, R0),
                      deps=[a1], tag=tag + "_musq")
            a4 = P.op("V", lambda v: v.tensor_sub(R1, R1, R2),
                      deps=[a2, a3], tag=tag + "_var")
            a6 = P.op("S", lambda s: s.activation(R1, R1, AF.Ln,
                                                  bias=eps_col[0:1, 0:1]),
                      deps=[a4], tag=tag + "_ln")
            a7 = P.op("S", lambda s: s.activation(R1, R1, AF.Exp, scale=-0.5),
                      deps=[a6], tag=tag + "_exp")
            a8 = P.op("V", lambda v: v.scalar_tensor_tensor(
                R2, R0, -1.0, R1, ALU.mult, ALU.mult), deps=[a7],
                tag=tag + "_c1")
            bt1 = bcast(R1, BC[:, :], 128, 4, deps=[a7, a8], tag=tag + "b1")
            BC2 = A5[:, 0:2048]
            bt2 = bcast(R2, BC2, 128, 5, deps=[a8], tag=tag + "b2")
            return bt1, bt2, BC2

        x_ring = [A4[:, 0:2048], A4[:, 2048:4096]]
        xc = []
        for t in range(NT):
            sl = x_ring[t % 2]
            dt_ = P.dma("G", sl, h["xT"][t * 128:(t + 1) * 128, :],
                        deps=[xc[t - 2]] if t >= 2 else (), tag="xload")
            ct = P.op("V", lambda v, o=resb_t[t], i=sl: v.tensor_copy(o, i),
                      deps=[dt_], tag="xcast")
            xc.append(ct)
        xb_ready = [P.tok("V")] + c_tok

        kvT = [A2[:, t * 1024:(t + 1) * 1024].bitcast(BF16) for t in range(NT)]
        spl = [A5[:, 0:1024].bitcast(BF16), A5[:, 1024:2048].bitcast(BF16)]
        spl_tok = [(), ()]

        def spill_evac(dst_hbm):
            def f(t, ps, tok):
                b = spl[t % 2]
                e = P.op("V", lambda v, b=b, ps=ps: v.tensor_copy(b, ps),
                         deps=[tok] + list(spl_tok[t % 2]), tag="spev")
                s_ = P.dma("G", dst_hbm[t], b, deps=[e], tag="spill")
                spl_tok[t % 2] = (s_,)
                return [e]
            return f

        def proj(out_cb, w_hbm, n_c, rhs_tiles, n_t, deps=(), tag="proj"):
            readers = [(), ()]
            for t in range(n_t):
                slot = t % 2
                ps = psA if slot == 0 else psB
                wbuf, wtok = wstream(w_hbm[t], n_c, slot, deps=deps)
                dd = [wtok] + list(deps) + list(readers[slot])
                tok = mm_group(ps, lambda c, wbuf=wbuf: wbuf[:, c * 128:(c + 1) * 128],
                               lambda c, ncx: rhs_tiles[c][:, ncx * 512:(ncx + 1) * 512],
                               n_c, dd, f"{tag}{t}")
                readers[slot] = out_cb(t, ps, tok)
                ws_state[slot] = (tok,)
            return readers

        proj(spill_evac(h["qT_sp"]), h["wq"], NT, resb_t, NT,
             deps=xb_ready, tag="q")

        def kv_proj():
            readers = [(), ()]
            for t in range(NT):
                slot = t % 2
                ps = psA if slot == 0 else psB
                stage1 = WS[:, slot * 1536: slot * 1536 + 1024]
                stage2 = A5[:, 2048 + slot * 1024: 3072 + slot * 1024]
                wbuf = WS[:, slot * 1536 + 1024: slot * 1536 + 1536].bitcast(BF16)
                tok1 = P.dma("Y", stage1.rearrange("p (c f) -> p c f", f=128),
                             h["wk"][t].rearrange("c p f -> p c f"),
                             deps=list(ws_state[slot]), tag="wk")
                tok2 = P.dma("Y", stage2.rearrange("p (c f) -> p c f", f=128),
                             h["wv"][t].rearrange("c p f -> p c f"),
                             tag="wv")
                ctok = P.op("V", lambda v, o=wbuf, a=stage1, b=stage2:
                            v.tensor_add(o, a, b), deps=[tok1, tok2], tag="kvadd")
                dd = [ctok] + list(readers[slot])
                tok = mm_group(ps, lambda c, wbuf=wbuf: wbuf[:, c * 128:(c + 1) * 128],
                               lambda c, ncx: resb_t[c][:, ncx * 512:(ncx + 1) * 512],
                               NT, dd, f"kv{t}")
                e = P.op("V", lambda v, o=kvT[t], i=ps: v.tensor_copy(o, i),
                         deps=[tok], tag="kvev")
                readers[slot] = [e]
                ws_state[slot] = (tok,)
        kv_proj()

        latb = [A5[:, 2048 + i * 1024: 3072 + i * 1024].bitcast(BF16)
                for i in range(2)]
        lat_deps = [P.tok("V")] + P.dtok("Y")

        def lat_evac(t, ps, tok):
            e = P.op("V", lambda v, o=latb[t], i=ps: v.tensor_copy(o, i),
                     deps=[tok], tag="latev")
            return [e]
        proj(lat_evac, h["wdown"], NT, kvT, 2, deps=lat_deps, tag="lat")
        lat_ready = [P.tok("V")]

        proj(spill_evac(h["kr_sp"]), h["wupk"], 2, latb, NT,
             deps=lat_ready, tag="kr")

        vext = A1[:, 0:8320].bitcast(BF16)
        vv4 = vext.rearrange("p (t h c) -> p t h c", t=16, h=16, c=65)
        ones_tok = P.op("V", lambda v: v.memset(vv4[:, :, :, 64:65], 1.0),
                        deps=lat_ready, tag="vext1")
        wupv_stage = A4[:, 0:2048]
        wupv_buf = A4[:, 2048:3072].bitcast(BF16)
        lv = P.dma("Y", wupv_stage.rearrange("p (c f) -> p c f", f=D),
                   h["wupv"].rearrange("c p f -> p c f"),
                   deps=lat_ready + [xc[-1]], tag="wupv")
        lvc = P.op("V", lambda v: v.tensor_copy(wupv_buf, wupv_stage),
                   deps=[lv], tag="wupvc")
        vrd = [(), ()]
        for tt in range(16):
            slot = tt % 2
            ps = (psA if slot == 0 else psB)[:, 0:1024]
            dd = [lvc, ones_tok] + list(vrd[slot])
            tok = None
            for nn in range(2):
                for c in range(2):
                    def mm(T, tt=tt, c=c, nn=nn, ps=ps):
                        return T.matmul(
                            ps[:, nn * 512:(nn + 1) * 512],
                            latb[c][:, tt * 128:(tt + 1) * 128],
                            wupv_buf[:, c * 1024 + nn * 512:
                                     c * 1024 + (nn + 1) * 512],
                            start=(c == 0), stop=(c == 1))
                    tok = P.op("T", mm, deps=dd if (nn == 0 and c == 0) else (),
                               inc=(1 if (nn == 1 and c == 1) else 0),
                               tag=f"vr{tt}")
            e = P.op("V", lambda v, tt=tt, ps=ps:
                     v.tensor_copy(vv4[:, tt, :, 0:64],
                                   ps.rearrange("p (hh c) -> p hh c", c=64)),
                     deps=[tok], tag=f"vrev{tt}")
            vrd[slot] = (e,)
        vext_ready = [P.tok("V")]

        decay_sb = A3[:, 0:4096]
        dtok = P.dma("Y", decay_sb, h["decay"][:], deps=lat_ready, tag="decay")

        qbuf = [A2[:, 0:1024].bitcast(BF16), A2[:, 1024:2048].bitcast(BF16)]
        krbuf = [A2[:, 2048:3072].bitcast(BF16), A2[:, 3072:4096].bitcast(BF16)]
        smask_ring = [A5[:, 0:2048], A5[:, 2048:4096]]
        expS_ring = [resb[:, i * 2048:(i + 1) * 2048] for i in range(3)]
        uraw = resb[:, 6144:10240].bitcast(F32)
        attnout = [A6[:, t * 1024:(t + 1) * 1024].bitcast(BF16) for t in range(4)] + \
                  [A2[:, 4096 + t * 1024: 5120 + t * 1024].bitcast(BF16)
                   for t in range(4)]

        pair_cons = [(), ()]
        ascr = resb[:, 10240:12288]
        prev_an = ()
        prev_anops = ()
        prev_bt = None
        prev_ash = None
        ed_p = A4[:, 0:2048]
        ed_m = A4[:, 2048:4096]
        edtok = [P.dma("Y", ed_p, h["edec"][0], deps=vext_ready, tag="edp"),
                 P.dma("Y", ed_m, h["edec"][1], deps=vext_ready, tag="edm")]
        qpb = [resb[:, 12288:14336]] * 2
        qmb = [resb[:, 14336:16384]] * 2
        kpb = [A5[:, i * 1024:(i + 1) * 1024].bitcast(BF16) for i in range(2)]
        kmb = [A5[:, 2048 + i * 1024:3072 + i * 1024].bitcast(BF16)
               for i in range(2)]
        scP4 = [psA[:, i * 512:(i + 1) * 512] for i in range(4)]
        expS8 = [resb[:, i * 512:(i + 1) * 512] for i in range(8)]
        uraw = resb[:, 6144:10240].bitcast(F32)
        sc_cons = {}
        es_cons = {}
        slot_vd = {}
        prep_cons = ()
        prev_uevac = None
        pendq = []
        self_last = [None]
        pidx = 0
        lds = {}

        def emit_ld(pt):
            slot = pt % 2
            l1 = P.dma("Y", qbuf[slot], h["qT_sp"][pt],
                       deps=list(pair_cons[slot]) + vext_ready + [dtok] +
                       edtok + list(spl_tok[0]) + list(spl_tok[1]),
                       tag=f"qld{pt}")
            l2 = P.dma("Y", krbuf[slot], h["kr_sp"][pt], tag=f"krld{pt}")
            lds[pt] = (l1, l2)

        emit_ld(0)
        for pt in range(8):
            slot = pt % 2
            if pt + 1 < 8:
                emit_ld(pt + 1)
            ld1, ld2 = lds[pt]
            u_lasts = []
            pv = []
            for dst, src_, e_ in ((qpb[slot], qbuf[slot], ed_p),
                                  (qmb[slot], qbuf[slot], ed_m),
                                  (kpb[slot], krbuf[slot], ed_p),
                                  (kmb[slot], krbuf[slot], ed_m)):
                pv.append(P.op("V", lambda v, dst=dst, src_=src_, e_=e_:
                               v.tensor_mul(dst, src_, e_),
                               deps=[ld1, ld2] + list(prep_cons),
                               tag=f"prep{pt}"))
            head_last = []
            for qh in range(2):
                for kc in range(16):
                    band_ci = kc // 4
                    brel = band_ci - 2 * qh
                    qq_seq = (1, 0) if brel == 0 else (0, 1)
                    etoks = {0: [], 1: []}
                    for qq_rel in qq_seq:
                        qq = 2 * qh + qq_rel
                        for hh in range(2):
                            hb = hh * 64
                            habs = 2 * pt + hh
                            sl4 = 2 * hh + qq_rel
                            sps = scP4[sl4]
                            dep0 = ([pv[-1]] +
                                    list(slot_vd.pop(sl4, ())) +
                                    list(sc_cons.get(sl4, ())))
                            vd = None
                            if qq == band_ci:
                                dloc = (kc % 4) * 128
                                def mmd(T, hb=hb, kc=kc, qq=qq, sps=sps,
                                        dloc=dloc, slot=slot):
                                    return T.matmul(
                                        sps[:, dloc:dloc + 128],
                                        krbuf[slot][hb:hb + 64,
                                                    kc * 128:(kc + 1) * 128],
                                        qbuf[slot][hb:hb + 64,
                                                   qq * 512 + dloc:
                                                   qq * 512 + dloc + 128],
                                        start=True, stop=True,
                                        tile_position=(hb, 0))
                                stok = P.op("T", mmd, deps=dep0, inc=1,
                                            tag=f"sc{habs}_{kc}_{qq}d")
                                for c0, c1, lh, rh in (
                                        (0, dloc, kmb[slot], qpb[slot]),
                                        (dloc + 128, 512,
                                         kpb[slot], qmb[slot])):
                                    if c1 <= c0:
                                        continue
                                    def mms(T, hb=hb, kc=kc, qq=qq, sps=sps,
                                            c0=c0, c1=c1, lh=lh, rh=rh):
                                        return T.matmul(
                                            sps[:, c0:c1],
                                            lh[hb:hb + 64,
                                               kc * 128:(kc + 1) * 128],
                                            rh[hb:hb + 64,
                                               qq * 512 + c0:qq * 512 + c1],
                                            start=True, stop=True,
                                            tile_position=(hb, 0))
                                    stok = P.op("T", mms, deps=(), inc=1,
                                                tag=f"sc{habs}_{kc}_{qq}s")
                                vd = P.op("V", lambda v, sps=sps, dloc=dloc:
                                          v.tensor_mul(
                                              sps[:, dloc:dloc + 128],
                                              sps[:, dloc:dloc + 128],
                                              decay_sb[:, 2048:2176]),
                                          deps=[stok], tag=f"bd{habs}_{kc}")
                                slot_vd[sl4] = (vd,)
                            else:
                                if qq > band_ci:
                                    lhs_t, rhs_t = kpb[slot], qmb[slot]
                                else:
                                    lhs_t, rhs_t = kmb[slot], qpb[slot]
                                def mm(T, hb=hb, kc=kc, qq=qq, lhs_t=lhs_t,
                                       rhs_t=rhs_t, sps=sps):
                                    return T.matmul(
                                        sps,
                                        lhs_t[hb:hb + 64,
                                              kc * 128:(kc + 1) * 128],
                                        rhs_t[hb:hb + 64,
                                              qq * 512:(qq + 1) * 512],
                                        start=True, stop=True,
                                        tile_position=(hb, 0))
                                stok = P.op("T", mm, deps=dep0,
                                            inc=1, tag=f"sc{habs}_{kc}_{qq}")
                            etoks[hh] += [stok, vd]
                            if qq_rel == qq_seq[1]:
                                habs_h = 2 * pt + hh
                                e0 = 2 * (pidx % 4)
                                dd_e = (etoks[hh] +
                                        [es_cons.get(e0),
                                         es_cons.get(e0 + 1)])
                                er_pair = resb[:, e0 * 512:e0 * 512 + 1024]
                                se = P.op(
                                    "S",
                                    lambda s, hh=hh, er_pair=er_pair:
                                    s.activation(er_pair,
                                                 psA[:, hh * 1024:
                                                     hh * 1024 + 1024],
                                                 AF.Exp),
                                    deps=dd_e, tag=f"ex{habs_h}_{kc}")
                                sc_cons[2 * hh] = (se,)
                                sc_cons[2 * hh + 1] = (se,)
                                if len(pendq) >= 2:
                                    pendq.pop(0)()

                                def uemit(kc=kc, habs=habs_h, hh=hh, se=se,
                                          pu=prev_uevac, qh=qh, e0=e0):
                                    for j in range(2):
                                        erj = expS8[e0 + j]
                                        bank = 2 * hh + j
                                        def mmU(T, kc=kc, habs=habs,
                                                bank=bank, erj=erj):
                                            return T.matmul(
                                                psB[0:65, bank * 512:
                                                    (bank + 1) * 512],
                                                vext[:, kc * 1040 + habs * 65:
                                                     kc * 1040 + habs * 65
                                                     + 65],
                                                erj,
                                                start=(kc == 0),
                                                stop=(kc == 15))
                                        tok = P.op(
                                            "T", mmU,
                                            deps=[se] + ([pu] if (kc == 0 and
                                                                  j == 0)
                                                         else []),
                                            inc=1, tag=f"U{habs}_{kc}_{j}")
                                        es_cons[e0 + j] = tok
                                        self_last[0] = tok
                                pendq.append(uemit)
                                pidx += 1
                while pendq:
                    pendq.pop(0)()
                u_last = self_last[0]
                ue = P.op("S", lambda s: s.copy(uraw[0:65, :], psB[0:65, :]),
                          deps=[u_last] + list(prev_anops),
                          tag=f"uev{pt}_{qh}")
                prev_uevac = ue
                sln = P.op("S", lambda s: s.activation(rowA[64:65, :],
                                                       uraw[64:65, :],
                                                       AF.Ln),
                           deps=[ue, prev_bt], tag=f"uln{pt}_{qh}")
                sex = P.op("S", lambda s: s.activation(rowA[64:65, :],
                                                       rowA[64:65, :],
                                                       AF.Exp, scale=-1.0),
                           deps=[sln], tag=f"uex{pt}_{qh}")
                bt = bcast(rowA[64:65, :], BC[0:64, :], 64, (2 * pt + qh) % 4,
                           deps=[sex] + list(prev_an), tag=f"ub{pt}_{qh}")
                prev_bt = bt
                qc0 = qh * 1024
                an_h0 = P.op("G", lambda g, pt=pt, qc0=qc0:
                             g.tensor_mul(
                                 attnout[pt][0:64, qc0:qc0 + 1024],
                                 uraw[0:64, 0:1024], BC[0:64, 0:1024]),
                             deps=[bt, ue], tag=f"an{pt}_{qh}")
                an0 = P.op("G", lambda g:
                           g.tensor_mul(ascr[0:64, 0:1024],
                                        uraw[0:64, 1024:2048],
                                        BC[0:64, 1024:2048]),
                           deps=[bt, ue, prev_ash], tag=f"an1{pt}_{qh}")
                an = P.dma("G", attnout[pt][64:128, qc0:qc0 + 1024],
                           ascr[0:64, 0:1024],
                           deps=[an0], tag=f"ash{pt}_{qh}")
                prev_ash = an
                prev_an = (an_h0, an)
                prev_anops = (an_h0, an0)
                head_last += [an_h0, an, u_last]
                u_lasts.append(u_last)
            pair_cons[slot] = tuple(pv) + tuple(u_lasts)
            prep_cons = tuple(u_lasts)
        att_done = [P.tok("V"), P.tok("S"), P.tok("T")] + P.dtok("G")

        sq_t = [A1[:, t * 1024:(t + 1) * 1024].bitcast(BF16) for t in range(NT)]
        y_ring = [A5[:, 0:2048], A5[:, 2048:4096]]
        y_sp = []
        wo_ring_cons = [tuple(att_done), tuple(att_done)]

        def wo_evac(t, ps, tok):
            sl = y_ring[t % 2]
            xl = P.dma("G", sl, h["xT"][t * 128:(t + 1) * 128, :],
                       deps=list(wo_ring_cons[t % 2]), tag=f"woxre{t}")
            ya = P.op("V", lambda v, sl=sl, ps=ps: v.tensor_add(sl, ps, sl),
                      deps=[tok, xl], tag=f"woadd{t}")
            yb = P.op("V", lambda v, t=t, sl=sl: v.tensor_copy(resb_t[t], sl),
                      deps=[ya], tag=f"wocast{t}")
            ysq = P.op("V", lambda v, t=t: v.tensor_mul(sq_t[t], resb_t[t],
                                                        resb_t[t]),
                       deps=[yb], tag=f"wosq{t}")
            sp = P.dma("G", h["x1_hbm"][t], sl, deps=[ya], tag=f"wosp{t}")
            y_sp.append(sp)
            wo_ring_cons[t % 2] = (ya, yb, ysq, sp)
            dump_tile("y1", t, sl, [ya])
            return [ya, yb, ysq]

        proj(wo_evac, h["wo"], NT, attnout, NT, deps=att_done, tag="wo")
        ln1_deps = [P.tok("V")] + y_sp

        def apply_ln(resid_src_hbm, out_bf_tiles, ln_deps, g_cols=None,
                     b_cols=None, out_f32_hbm=None, shifted=False,
                     dump_name=None, tag="ln"):
            stok = ln_stats(stats_src[0], stats_src[1], ln_deps)
            bt1, bt2, BC2 = ln_rows(stok, tag)
            ring = ln_ring
            nr = len(ring)
            prevs = [() for _ in range(nr)]
            toks = []
            for t in range(NT):
                sl = ring[t % nr]
                if shifted and t >= 4:
                    xl = P.dma("G", sl[:, 1:S], resid_src_hbm[t][:, 0:S - 1],
                               deps=[stok] + list(prevs[t % nr]),
                               tag=f"{tag}re{t}")
                    zt = P.op("S", lambda s, sl=sl: s.memzero(sl[:, 0:1]),
                              deps=[xl] + list(prevs[t % nr]), tag=f"{tag}z{t}")
                else:
                    xl = P.dma("G", sl, resid_src_hbm[t],
                               deps=[stok] + list(prevs[t % nr]),
                               tag=f"{tag}re{t}")
                    zt = None
                m1 = P.op("V", lambda v, sl=sl: v.tensor_mul(sl, sl, BC[:, :]),
                          deps=[xl, bt1, zt] + list(prevs[t % nr]),
                          tag=f"{tag}m{t}")
                m2 = P.op("V", lambda v, sl=sl: v.tensor_add(sl, sl, BC2),
                          deps=[m1, bt2], tag=f"{tag}a{t}")
                if g_cols is not None:
                    m3 = P.op("V", lambda v, sl=sl, t=t: v.tensor_scalar(
                        out_bf_tiles[t], sl, g_cols[t], b_cols[t],
                        ALU.mult, ALU.add), deps=[m2], tag=f"{tag}g{t}")
                else:
                    m3 = P.op("V", lambda v, sl=sl, t=t:
                              v.tensor_copy(out_bf_tiles[t], sl),
                              deps=[m2], tag=f"{tag}c{t}")
                last = (m3,)
                if out_f32_hbm is not None:
                    sp = P.dma("G", out_f32_hbm[t], sl, deps=[m2],
                               tag=f"{tag}sp{t}")
                    toks.append(sp)
                    last = (m3, sp)
                if dump_name:
                    dump_tile(dump_name, t, sl, [m2])
                prevs[t % nr] = last
                toks.append(m3)
            return toks

        ln_ring = [A4[:, 2048:4096], A5[:, 2048:4096],
                   A6[:, 0:2048], A6[:, 2048:4096]]
        stats_src = (resb_t, sq_t)
        ln1_toks = apply_ln(h["x1_hbm"], resb_t, ln1_deps,
                            out_f32_hbm=h["x1_hbm"], dump_name="x1", tag="l1")

        if upto == "mla":
            prevs = ()
            for t in range(NT):
                sl = A4[:, 0:2048]
                xl = P.dma("G", sl, h["x1_hbm"][t], deps=ln1_toks + list(prevs),
                           tag=f"dbgre{t}")
                od = P.dma("G", h["outT"][t * 128:(t + 1) * 128, :], sl,
                           deps=[xl], tag=f"dbgout{t}")
                prevs = (od,)
            P.join(engs=("G",))
            P.emit(blk)
            return nc

        def ffn_iter(l, in_hbm, mid_hbm, last_toks, stop_at=None):
            sh_tiles = [A1[:, i * 1024:(i + 1) * 1024].bitcast(BF16)
                        for i in range(4)]
            shtoks = []
            for i in range(4):
                t1 = P.op("V", lambda v, i=i: v.tensor_copy(
                    sh_tiles[i][:, 1:S], resb_t[4 + i][:, 0:S - 1]),
                    deps=last_toks, tag=f"sh{i}")
                t2 = P.op("V", lambda v, i=i: v.memset(sh_tiles[i][:, 0:1], 0.0),
                          deps=[t1], tag=f"shz{i}")
                shtoks.append(t2)
            rhs_h = resb_t[0:4] + sh_tiles

            qtok = P.dma("Y", qs_col, h["fqs"][l], tag="qs")
            ktok = P.dma("Y", ks_col, h["fks"][l], tag="ks")
            cmap = [A3[:, 0:2048], A3[:, 2048:4096]]
            m_tok1 = P.dma("Y", cmap[0], h["rot_qc1"][:], deps=last_toks, tag="c1")
            m_tok2 = P.dma("Y", cmap[1], h["rot_qc2"][:], tag="c2")

            qrot = [A2[:, t * 1024:(t + 1) * 1024].bitcast(BF16) for t in range(4)]
            krot = [A2[:, 4096 + t * 1024: 5120 + t * 1024].bitcast(BF16)
                    for t in range(4)]
            qraw_ring = [A5[:, 0:2048], A5[:, 2048:4096]]
            T32_ring = [A6[:, 0:2048], A6[:, 2048:4096]]
            q2b_ring = [A4[:, 0:1024].bitcast(BF16), A4[:, 1024:2048].bitcast(BF16)]
            Tb_ring = [A4[:, 2048:3072].bitcast(BF16)] * 2
            PTb_ring = [A4[:, 3072:4096].bitcast(BF16)] * 2
            u_sc = [A1[:, 4096:6144], A1[:, 6144:8192]]
            lvl = {"f1a": 1, "f1b": 2, "f1c": 3}.get(stop_at, 9)
            psA_cons = ()
            stats_cons = ()
            tb_cons = ()
            ptb_cons = ()
            ring_cons = [(), ()]
            rot_toks = []
            for t in range(NT):
                is_q = t < 4
                slot = t % 2
                if t == 4:
                    m_tok1 = P.dma("Y", cmap[0], h["rot_kc3"][:],
                                   deps=list(ring_cons[0]) + list(ring_cons[1]),
                                   tag="c3")
                    m_tok2 = P.dma("Y", cmap[1], h["rot_kc4"][:], tag="c4")
                wbuf, wtok = wstream(h["fwqk"][l][t], NT, slot,
                                     deps=shtoks if t < 2 else ())
                dd = [wtok, qtok, ktok, m_tok1, m_tok2] + shtoks + list(psA_cons)
                tok = mm_group(psA, lambda c, wbuf=wbuf: wbuf[:, c * 128:(c + 1) * 128],
                               lambda c, ncx: rhs_h[c][:, ncx * 512:(ncx + 1) * 512],
                               NT, dd, f"fqkv{t}")
                ws_state[slot] = (tok,)
                qraw = qraw_ring[slot]
                e1 = P.op("V", lambda v, qraw=qraw: v.tensor_copy(qraw, psA),
                          deps=[tok] + list(ring_cons[slot]), tag=f"fqr{t}")
                q2b = q2b_ring[slot]
                e2 = P.op("S", lambda s, q2b=q2b, qraw=qraw:
                          s.activation(q2b, qraw, AF.Square),
                          deps=[e1], tag=f"fsq{t}")
                psA_cons = (e1,)
                if lvl < 2:
                    ring_cons[slot] = (e1, e2)
                    continue
                sstok = None
                for hh in range(2):
                    for ncx in range(4):
                        def mm(T, hh=hh, ncx=ncx, q2b=q2b):
                            return T.matmul(
                                psB[32 * hh:32 * hh + 1, ncx * 512:(ncx + 1) * 512],
                                ones_bf[hh * 64:hh * 64 + 64, 0:1],
                                q2b[hh * 64:hh * 64 + 64, ncx * 512:(ncx + 1) * 512],
                                start=True, stop=True,
                                tile_position=(hh * 64, hh * 32))
                        sstok = P.op("T", mm,
                                     deps=([e2] + list(stats_cons))
                                     if (hh == 0 and ncx == 0) else (),
                                     inc=(1 if (hh == 1 and ncx == 3) else 0),
                                     tag=f"fss{t}")
                r_toks = []
                for hh, base in ((0, 0), (1, 32)):
                    r1 = P.op("V", lambda v, base=base: v.tensor_scalar_max(
                        rowA[base:base + 1, :], psB[base:base + 1, :], 1e-24),
                        deps=[sstok], tag=f"frn{t}_{hh}")
                    r2 = P.op("S", lambda s, base=base: s.activation(
                        rowA[base:base + 1, :], rowA[base:base + 1, :], AF.Ln),
                        deps=[r1], tag=f"frl{t}_{hh}")
                    r3 = P.op("S", lambda s, base=base: s.activation(
                        rowA[base:base + 1, :], rowA[base:base + 1, :], AF.Exp,
                        scale=-0.5), deps=[r2], tag=f"fre{t}_{hh}")
                    b_ = bcast(rowA[base:base + 1, :], BC[hh * 64:hh * 64 + 64, :],
                               64, 6 + hh, deps=[r3], tag=f"fbc{t}_{hh}")
                    r_toks.append((r1, b_))
                stats_cons = (r_toks[0][0], r_toks[1][0])
                if lvl < 3:
                    ring_cons[slot] = (e1, e2)
                    continue
                sc = qs_col if is_q else ks_col
                T32 = T32_ring[slot]
                v1 = P.op("V", lambda v, T32=T32, qraw=qraw, sc=sc:
                          v.scalar_tensor_tensor(T32, BC[:, :], sc[:, 0:1],
                                                 qraw, ALU.mult, ALU.mult),
                          deps=[r_toks[0][1], r_toks[1][1], e1], tag=f"fT{t}")
                Tb = Tb_ring[slot]
                v2 = P.op("S", lambda s, Tb=Tb, T32=T32: s.copy(Tb, T32),
                          deps=[v1] + list(tb_cons), tag=f"fTb{t}")
                PTb = PTb_ring[slot]
                rs = []
                for (a, b) in ((0, 32), (32, 0), (64, 96), (96, 64)):
                    rs.append(P.dma("Y", PTb[a:a + 32, :], Tb[b:b + 32, :],
                                    deps=[v2] + list(ptb_cons), tag=f"frs{t}"))
                usc = u_sc[slot]
                v3 = P.op("V", lambda v, usc=usc, PTb=PTb:
                          v.tensor_mul(usc, PTb, cmap[1]), deps=rs, tag=f"fu{t}")
                tb_cons = tuple(rs)
                ptb_cons = (v3,)
                v4 = P.op("V", lambda v, T32=T32: v.tensor_mul(T32, T32, cmap[0]),
                          deps=[v1, v2, m_tok1], tag=f"fm{t}")
                dst = (qrot if is_q else krot)[t % 4]
                v5 = P.op("V", lambda v, dst=dst, usc=usc, T32=T32:
                          v.tensor_add(dst, T32, usc), deps=[v3, v4],
                          tag=f"fr{t}")
                ring_cons[slot] = (v5, v2)
                rot_toks.append(v5)
            rot_done = [P.tok("V"), P.tok("S")]
            if lvl < 9:
                return rot_done + [P.tok("T")] + P.dtok("G") + P.dtok("Y")

            vrow = A6[:, 0:4096].bitcast(BF16)
            fwv_stage = A4[:, 0:2048]
            fwv_buf = A4[:, 2048:4096].bitcast(BF16)
            fl = h["fwv"][l].rearrange("c p f -> p c f")
            st3 = fwv_stage.rearrange("p (c f) -> p c f", f=512)
            lv1 = P.dma("Y", st3, fl[:, 0:4, :], deps=rot_done, tag="fwv0")
            lc1 = P.op("V", lambda v: v.tensor_copy(fwv_buf[:, 0:2048], fwv_stage),
                       deps=[lv1], tag="fwvc0")
            lv2 = P.dma("Y", st3, fl[:, 4:8, :], deps=[lc1], tag="fwv1")
            lc2 = P.op("V", lambda v: v.tensor_copy(fwv_buf[:, 2048:4096],
                                                    fwv_stage),
                       deps=[lv2], tag="fwvc1")
            vrd = [(), ()]
            for tt in range(16):
                slot = tt % 2
                ps = (psA if slot == 0 else psB)[:, 0:512]
                dd = [lc1, lc2] + list(vrd[slot]) + rot_done
                tok = None
                for c in range(NT):
                    def mm(T, tt=tt, c=c, ps=ps):
                        return T.matmul(ps, rhs_h[c][:, tt * 128:(tt + 1) * 128],
                                        fwv_buf[:, c * 512:(c + 1) * 512],
                                        start=(c == 0), stop=(c == NT - 1))
                    tok = P.op("T", mm, deps=dd if c == 0 else (),
                               inc=(1 if c == NT - 1 else 0), tag=f"fv{tt}")
                e = P.op("V", lambda v, tt=tt, ps=ps:
                         v.tensor_copy(vrow[:, tt * 512:(tt + 1) * 512], ps),
                         deps=[tok], tag=f"fvv{tt}")
                vrd[slot] = (e,)
            v_done = [P.tok("V")]
            if stop_at == "f1":
                return v_done + rot_done

            mtok = P.dma("Y", A3[:, 0:3200].rearrange("p (m f) -> p m f", f=640),
                         h["lmask"].rearrange("m p f -> p m f"),
                         deps=v_done, tag="lmask")
            masks = [A3[:, i * 640:(i + 1) * 640] for i in range(5)]
            o_row = A5[:, 0:4096].bitcast(BF16)
            smask_r = [A4[:, 0:640], A4[:, 640:1280]]
            ae_r = [A4[:, 1280 + i * 320: 1600 + i * 320].bitcast(BF16)
                    for i in range(2)]
            aeT_r = [A4[:, 1920 + i * 320: 2240 + i * 320].bitcast(BF16)
                     for i in range(2)]
            simP = [psA[:, 0:640], psA[:, 1024:1664]]
            outP = [psA[:, 768:832], psA[:, 1792:1856]]
            aeTP = [psB[:, 0:640], psB[:, 1024:1664]]
            grp_cons = [v_done, v_done]
            aeT_cons = [(), ()]
            sm_war = {}
            ae_war = {}
            rc_war = {}
            pend2 = []
            for ui in range(128):
                qt, hh = ui // 8, ui % 8
                g = ui % 2
                ftile = hh // 2
                hb = (hh % 2) * 64
                kw = max(0, qt * 128 - 512)
                mask = masks[qt] if qt < 4 else masks[4]
                dd = [mtok] + list(grp_cons[g])
                stok = None
                for part in range(2):
                    lo, hi = part * 512, min(640, (part + 1) * 512)
                    def mm(T, ftile=ftile, hb=hb, qt=qt, kw=kw, lo=lo, hi=hi, g=g):
                        return T.matmul(
                            simP[g][:, lo:hi],
                            qrot[ftile][hb:hb + 64, qt * 128:(qt + 1) * 128],
                            krot[ftile][hb:hb + 64, kw + lo:kw + hi],
                            start=True, stop=True, tile_position=(hb, 0))
                    stok = P.op("T", mm, deps=dd if part == 0 else (),
                                inc=(part == 1), tag=f"si{ui}")
                sm = smask_r[g]
                m1 = P.op("V", lambda v, g=g, sm=sm, mask=mask:
                          v.tensor_add(sm, simP[g], mask),
                          deps=[stok, sm_war.get(g)], tag=f"lm{ui}")
                nm = nm_cols[g]
                m2 = P.op("V", lambda v, sm=sm, nm=nm:
                          v.reduce_max(nm[:, 0:1], sm, axis=AXX, negate=True),
                          deps=[m1], tag=f"lx{ui}")
                while len(pend2) >= 2:
                    pend2.pop(0)()
                ae = ae_r[g]
                rc = rc_cols[g]
                m3 = P.op("S", lambda s, sm=sm, ae=ae, nm=nm, rc=rc:
                          s.activation(ae, sm, AF.Exp, bias=nm[:, 0:1],
                                       accum_out=rc[:, 0:1]),
                          deps=[m2, ae_war.get(g), rc_war.get(g)],
                          tag=f"le{ui}")
                sm_war[g] = m3
                grp_cons[g] = (m1,)

                def tail(ui=ui, qt=qt, hh=hh, g=g, kw=kw, ae=ae, rc=rc, m3=m3):
                    dd2 = [m3] + list(aeT_cons[g])
                    ttok = None
                    for b in range(5):
                        def mm(T, ae=ae, b=b, g=g):
                            return T.matmul(aeTP[g][:, b * 128:(b + 1) * 128],
                                            ae[:, b * 128:(b + 1) * 128],
                                            ident_bf[:, 0:128],
                                            start=True, stop=True)
                        ttok = P.op("T", mm, deps=dd2 if b == 0 else (),
                                    inc=(b == 4), tag=f"lt{ui}")
                    ae_war[g] = ttok
                    aeT = aeT_r[g]
                    m4 = P.op("S", lambda s, g=g, aeT=aeT: s.copy(aeT, aeTP[g]),
                              deps=[ttok], tag=f"la{ui}")
                    aeT_cons[g] = (m4,)
                    otok = None
                    for b in range(5):
                        tv = kw // 128 + b
                        def mm(T, aeT=aeT, b=b, g=g, tv=tv, hh=hh):
                            return T.matmul(
                                outP[g], aeT[:, b * 128:(b + 1) * 128],
                                vrow[:, tv * 512 + hh * 64:
                                     tv * 512 + hh * 64 + 64],
                                start=(b == 0), stop=(b == 4))
                        otok = P.op("T", mm, deps=[m4] if b == 0 else (),
                                    inc=(b == 4), tag=f"lo{ui}")
                    m5 = P.op("V", lambda v, rc=rc: v.reciprocal(rc[:, 0:1],
                                                                 rc[:, 0:1]),
                              deps=[m3], tag=f"lr{ui}")
                    m6 = P.op("V", lambda v, g=g, qt=qt, hh=hh, rc=rc:
                              v.tensor_scalar_mul(
                                  o_row[:, qt * 512 + hh * 64:
                                        qt * 512 + hh * 64 + 64],
                                  outP[g], rc[:, 0:1]),
                              deps=[otok, m5], tag=f"ln{ui}")
                    rc_war[g] = m6
                    grp_cons[g] = grp_cons[g] + (m6,)
                pend2.append(tail)
            while pend2:
                pend2.pop(0)()
            lat_done = [P.tok("V"), P.tok("S")]
            if stop_at == "f2":
                return lat_done

            oT = [A2[:, t * 1024:(t + 1) * 1024].bitcast(BF16) for t in range(4)]
            otr_cons = [lat_done, lat_done]
            for ft in range(4):
                for tt in range(16):
                    g = tt % 2
                    ps = (psA if g == 0 else psB)[:, 0:128]
                    def mm(T, ft=ft, tt=tt, ps=ps):
                        return T.matmul(ps, o_row[:, tt * 512 + ft * 128:
                                                  tt * 512 + (ft + 1) * 128],
                                        ident_bf[:, 0:128], start=True, stop=True)
                    tok = P.op("T", mm, deps=list(otr_cons[g]) + lat_done,
                               inc=1, tag=f"ot{ft}_{tt}")
                    e = P.op("V", lambda v, ft=ft, tt=tt, ps=ps:
                             v.tensor_copy(oT[ft][:, tt * 128:(tt + 1) * 128], ps),
                             deps=[tok], tag=f"ote{ft}_{tt}")
                    otr_cons[g] = (e,)
            oT_done = [P.tok("V")]

            sq2 = [A1[:, t * 1024:(t + 1) * 1024].bitcast(BF16) for t in range(NT)]
            y_ring2 = [A5[:, 0:2048], A5[:, 2048:4096]]
            sp2 = []
            fwo_ring_cons = [tuple(oT_done), tuple(oT_done)]

            def fwo_evac(t, ps, tok):
                sl = y_ring2[t % 2]
                xl = P.dma("G", sl, in_hbm[t],
                           deps=list(fwo_ring_cons[t % 2]), tag=f"fwox{t}")
                ya = P.op("V", lambda v, sl=sl, ps=ps: v.tensor_add(sl, ps, sl),
                          deps=[tok, xl], tag=f"fwoa{t}")
                yb = P.op("V", lambda v, t=t, sl=sl: v.tensor_copy(resb_t[t], sl),
                          deps=[ya], tag=f"fwoc{t}")
                ysq = P.op("V", lambda v, t=t: v.tensor_mul(sq2[t], resb_t[t],
                                                            resb_t[t]),
                           deps=[yb], tag=f"fwos{t}")
                sp = P.dma("G", mid_hbm[t], sl, deps=[ya], tag=f"fwosp{t}")
                sp2.append(sp)
                fwo_ring_cons[t % 2] = (ya, yb, ysq, sp)
                dump_tile("x2_%d" % l, t, sl, [ya])
                return [ya, yb, ysq]

            if "fwo" in joins:
                P.join()
            proj(fwo_evac, h["fwo"][l], 4, oT, NT, deps=oT_done, tag="fwo")
            x2_done = [P.tok("V")] + sp2
            if stop_at == "f3":
                return x2_done

            s2b = [A6[:, i * 1024:(i + 1) * 1024].bitcast(BF16) for i in range(4)]
            s2toks = []
            for i in range(4):
                t1 = P.op("V", lambda v, i=i: v.tensor_copy(
                    s2b[i][:, 1:S], resb_t[4 + i][:, 0:S - 1]),
                    deps=x2_done, tag=f"s2{i}")
                t2 = P.op("V", lambda v, i=i: v.memset(s2b[i][:, 0:1], 0.0),
                          deps=[t1], tag=f"s2z{i}")
                s2toks.append(t2)
            h2 = resb_t[0:4] + s2b
            sq2toks = []
            for t in range(4, NT):
                tk = P.op("V", lambda v, t=t: v.tensor_mul(sq2[t], h2[t], h2[t]),
                          deps=[s2toks[t - 4]], tag=f"s2q{t}")
                sq2toks.append(tk)
            gtok = P.dma("Y", lng_cols, h["flng"][l], tag="lng")
            btok = P.dma("Y", lnb_cols, h["flnb"][l], tag="lnb")
            g_cols = [lng_cols[:, t:t + 1] for t in range(NT)]
            b_cols = [lnb_cols[:, t:t + 1] for t in range(NT)]
            lnb_tiles = [A2[:, t * 1024:(t + 1) * 1024].bitcast(BF16)
                         for t in range(NT)]
            globals()["stats_src"] = (h2, sq2)
            stats_src2 = (h2, sq2)
            if "ln2" in joins:
                P.join()
            stok = ln_stats(h2, sq2, x2_done + s2toks + sq2toks + [gtok, btok])
            bt1, bt2, BC2 = ln_rows(stok, f"l2_{l}")
            prevs = [(), ()]
            ln2_toks = []
            ring2x = [A4[:, 0:2048], A6[:, 0:2048],
                      A5[:, 2048:4096], A4[:, 2048:4096]]
            nr2 = 4
            prevs = [() for _ in range(nr2)]
            for t in range(NT):
                sl = ring2x[t % nr2]
                if t >= 4:
                    xl = P.dma("G", sl[:, 1:S], mid_hbm[t][:, 0:S - 1],
                               deps=[stok] + list(prevs[t % nr2]),
                               tag=f"l2re{t}")
                    zt = P.op("S", lambda s, sl=sl: s.memzero(sl[:, 0:1]),
                              deps=[xl] + list(prevs[t % nr2]), tag=f"l2z{t}")
                else:
                    xl = P.dma("G", sl, mid_hbm[t],
                               deps=[stok] + list(prevs[t % nr2]),
                               tag=f"l2re{t}")
                    zt = None
                m1 = P.op("V", lambda v, sl=sl: v.tensor_mul(sl, sl, BC[:, :]),
                          deps=[xl, bt1, zt] + list(prevs[t % nr2]),
                          tag=f"l2m{t}")
                m2 = P.op("V", lambda v, sl=sl: v.tensor_add(sl, sl, BC2),
                          deps=[m1, bt2], tag=f"l2a{t}")
                dump_tile("ln2pre", t, sl, [m2])
                m3 = P.op("V", lambda v, sl=sl, t=t: v.tensor_scalar(
                    lnb_tiles[t], sl, g_cols[t], b_cols[t], ALU.mult, ALU.add),
                    deps=[m2], tag=f"l2g{t}")
                prevs[t % nr2] = (m3,)
                ln2_toks.append(m3)
            if stop_at == "ln2":
                prevs2 = ()
                for t in range(NT):
                    sl = A4[:, 0:2048]
                    cv = P.op("V", lambda v, t=t, sl=sl:
                              v.tensor_copy(sl, lnb_tiles[t]),
                              deps=ln2_toks + list(prevs2), tag=f"lnd{t}")
                    od = P.dma("G", h["outT"][t * 128:(t + 1) * 128, :], sl,
                               deps=[cv], tag=f"lno{t}")
                    prevs2 = (od,)
                return P.dtok("G")

            if "lnbd" in dumps:
                dprev = ()
                for t in range(NT):
                    sl = BC[:, 0:2048]
                    cv = P.op("V", lambda v, t=t, sl=sl:
                              v.tensor_copy(sl, lnb_tiles[t]),
                              deps=ln2_toks + list(dprev), tag=f"lnbd{t}")
                    od = P.dma("G", dumps["lnbd"][t * 128:(t + 1) * 128, :], sl,
                               deps=[cv], tag=f"lnbo{t}")
                    dprev = (od,)

            if "ln2w1" in joins:
                P.join()
            b1t = [P.dma("Y", b1a_cols, h["fb1a"][l], tag="b1a"),
                   P.dma("Y", b1g_cols, h["fb1g"][l], tag="b1g"),
                   P.dma("Y", b2_cols, h["fb2"][l], tag="b2")]
            b1s = P.op("V", lambda v: v.tensor_scalar_mul(b1gs_cols, b1g_cols,
                                                          SQ2I),
                       deps=[b1t[1]], tag="b1gs")
            gated = [A1[:, j * 512:(j + 1) * 512].bitcast(BF16)
                     for j in range(16)] + \
                    [A3[:, j * 512:(j + 1) * 512].bitcast(BF16) for j in range(6)]
            a_r = [A4[:, 2048 + i * 1024: 3072 + i * 1024] for i in range(2)]
            e_r = [A5[:, i * 1024:(i + 1) * 1024] for i in range(2)]
            g_r = [A5[:, 2048 + i * 1024: 3072 + i * 1024] for i in range(2)]
            y2_r = [A4[:, 0:2048], A6[:, 2048:4096]]
            w2tmp = A6[:, 0:2048]
            w2_out_toks = []
            resid_by_tile = [[] for _ in range(NT)]
            gate_cons = [(), ()]
            w2tmp_cons = ()
            w2_cons = [tuple(prevs[0]), tuple(prevs[1])]
            psA_w1 = ()
            psB_w1 = ()
            resid_toks = []
            for half in range(2):
                off = half * 1024
                half_gtoks = []
                for j in range(NTH):
                    slot = j % 2
                    ar, er, gr = a_r[slot], e_r[slot], g_r[slot]
                    wbuf, wtok = wstream(h["fw1a"][l][j], NT, 0,
                                         deps=ln2_toks if (half == 0 and j < 2)
                                         else ())
                    dd = [wtok, b1t[0], b1s] + ln2_toks + \
                        list(gate_cons[slot]) + list(psA_w1) + \
                        list(w2_cons[0]) + list(w2_cons[1])
                    tok = mm_group(
                        psA[:, 0:1024],
                        lambda c, wbuf=wbuf: wbuf[:, c * 128:(c + 1) * 128],
                        lambda c, ncx, off=off: lnb_tiles[c][:, off + ncx * 512:
                                                             off + (ncx + 1) * 512],
                        NT, dd, f"w1a{half}_{j}", n_nc=2)
                    ws_state[0] = (tok,)
                    ea = P.op("S", lambda s, ar=ar, j=j: s.activation(
                        ar[:, 0:1024], psA[:, 0:1024], AF.Identity,
                        bias=b1a_cols[:, j:j + 1]),
                        deps=[tok], tag=f"w1ae{half}_{j}")
                    psA_w1 = (ea,)
                    wbuf2, wtok2 = wstream(h["fw1g"][l][j], NT, 1, deps=())
                    dd2 = [wtok2] + list(psB_w1) + \
                        list(w2_cons[0]) + list(w2_cons[1])
                    tok2 = mm_group(
                        psB[:, 0:1024],
                        lambda c, wbuf2=wbuf2: wbuf2[:, c * 128:(c + 1) * 128],
                        lambda c, ncx, off=off: lnb_tiles[c][:, off + ncx * 512:
                                                             off + (ncx + 1) * 512],
                        NT, dd2, f"w1g{half}_{j}", n_nc=2)
                    ws_state[1] = (tok2,)
                    ee = P.op("S", lambda s, er=er, j=j: s.activation(
                        er[:, 0:1024], psB[:, 0:1024], AF.Gelu,
                        bias=b1g_cols[:, j:j + 1]),
                        deps=[tok2], tag=f"w1gl{half}_{j}")
                    psB_w1 = (ee,)
                    g2 = P.op("V", lambda v, ar=ar, er=er, j=j:
                              v.tensor_mul(gated[j], ar[:, 0:1024], er[:, 0:1024]),
                              deps=[ee] + list(w2_cons[0]) + list(w2_cons[1]),
                              tag=f"w1o{half}_{j}")
                    if half == 0 and j == 0 and "w1dbg" in dumps:
                        dbg = dumps["w1dbg"]
                        d1 = P.dma("G", dbg[0:128, 0:1024], ar[:, 0:1024],
                                   deps=[g2], tag="dbg_a")
                        d2 = P.dma("G", dbg[128:256, 0:1024], gr[:, 0:1024],
                                   deps=[g2], tag="dbg_g")
                        d3 = P.dma("G", dbg[256:384, 0:1024], er[:, 0:1024],
                                   deps=[g2], tag="dbg_e")
                        cv = P.op("V", lambda v: v.tensor_copy(
                            BC[:, 0:1024], gated[0][:, 0:1024]),
                            deps=[g2], tag="dbg_gc")
                        d4 = P.dma("G", dbg[384:512, 0:1024], BC[:, 0:1024],
                                   deps=[cv], tag="dbg_gt")
                        gate_cons[slot] = (g2, ea, d1, d2, d3, d4)
                    else:
                        gate_cons[slot] = (g2, ea)
                    half_gtoks.append(g2)
                if "w1w2" in joins:
                    P.join()
                for e in ("Y", "G", "T", "V", "S"):
                    if ("w1w2" + e) in joins:
                        P.join(engs=(e,))
                for t in range(NT):
                    slot = t % 2
                    ps = psA if slot == 0 else psB
                    tok = None
                    for grp in range(3):
                        c0, c1 = grp * 8, min(NTH, grp * 8 + 8)
                        wbuf, wtok = wstream(h["fw2"][l][t][c0:c1], c1 - c0,
                                             (t + grp) % 2,
                                             deps=half_gtoks[-1:])
                        dd = [wtok] + half_gtoks + list(w2_cons[slot]) + b1t
                        tok = mm_group(
                            ps[:, 0:1024],
                            lambda ci, wbuf=wbuf: wbuf[:, ci * 128:(ci + 1) * 128],
                            lambda ci, ncx, c0=c0: gated[c0 + ci][:, ncx * 512:
                                                                  (ncx + 1) * 512],
                            c1 - c0, dd, f"w2_{half}_{t}_{grp}", n_nc=2,
                            start_c=(0 if grp == 0 else -1),
                            stop_c=(c1 - c0 - 1 if grp == 2 else -2))
                        ws_state[slot] = (tok,)
                    sl = y2_r[t % 2]
                    xl = P.dma("G", sl[:, 0:1024], mid_hbm[t][:, off:off + 1024],
                               deps=list(w2_cons[t % 2]), tag=f"w2x{half}_{t}")
                    ya = P.op("V", lambda v, ps=ps, sl=sl: v.scalar_tensor_tensor(
                        sl[:, 0:1024], ps[:, 0:1024], 1.0, sl[:, 0:1024],
                        ALU.mult, ALU.add),
                        deps=[tok, xl] + list(w2_cons[t % 2]),
                        tag=f"w2b{half}_{t}")
                    yb = P.op("V", lambda v, sl=sl, t=t: v.tensor_scalar_add(
                        sl[:, 0:1024], sl[:, 0:1024], b2_cols[:, t:t + 1]),
                        deps=[ya], tag=f"w2a{half}_{t}")
                    yc = P.op("V", lambda v, t=t, sl=sl, off=off: v.tensor_copy(
                        resb_t[t][:, off:off + 1024], sl[:, 0:1024]),
                        deps=[yb], tag=f"w2c{half}_{t}")
                    sp = P.dma("G", mid_hbm[t][:, off:off + 1024], sl[:, 0:1024],
                               deps=[yb], tag=f"w2sp{half}_{t}")
                    resid_toks.append(sp)
                    resid_by_tile[t].append(sp)
                    w2_cons[t % 2] = (yb, yc, ya, sp)
            if "lnbd2" in dumps:
                dprev = ()
                cur = [P.tok("V"), P.tok("S"), P.tok("T")] + P.dtok("G")
                for t in range(NT):
                    sl = BC[:, 0:2048]
                    cv = P.op("V", lambda v, t=t, sl=sl:
                              v.tensor_copy(sl, lnb_tiles[t]),
                              deps=cur + list(dprev), tag=f"ln2d{t}")
                    od = P.dma("G", dumps["lnbd2"][t * 128:(t + 1) * 128, :], sl,
                               deps=[cv], tag=f"ln2o{t}")
                    dprev = (od,)
            it_ret = [P.tok("V"), P.tok("S")] + resid_toks
            it_ret_by_tile[:] = [list(x) for x in resid_by_tile]
            return it_ret

        if upto in ("f1", "f2", "f3", "f1a", "f1b", "f1c", "ln2"):
            f3t = ffn_iter(0, h["x1_hbm"], h["res_hbm"], ln1_toks, stop_at=upto)
            prevs = ()
            for t in (range(NT) if upto != "ln2" else []):
                sl = A4[:, 0:2048]
                xl = P.dma("G", sl, h["res_hbm"][t], deps=f3t + list(prevs),
                           tag=f"dre{t}")
                od = P.dma("G", h["outT"][t * 128:(t + 1) * 128, :], sl,
                           deps=[xl], tag=f"dout{t}")
                prevs = (od,)
            P.join(engs=("G",))
            P.emit(blk)
            return nc
        it_ret_by_tile = [[] for _ in range(NT)]
        it1 = ffn_iter(0, h["x1_hbm"], h["res_hbm"], ln1_toks)
        if upto == "it1":
            prevs = ()
            for t in range(NT):
                sl = A4[:, 0:2048]
                xl = P.dma("G", sl, h["res_hbm"][t], deps=it1 + list(prevs),
                           tag=f"dre{t}")
                od = P.dma("G", h["outT"][t * 128:(t + 1) * 128, :], sl,
                           deps=[xl], tag=f"dout{t}")
                prevs = (od,)
            P.join(engs=("G",))
            P.emit(blk)
            return nc

        it2 = ffn_iter(1, h["res_hbm"], h["res_hbm"], it1)

        zb = [A2[:, t * 1024:(t + 1) * 1024].bitcast(BF16) for t in range(NT)]
        zsq = [A1[:, t * 1024:(t + 1) * 1024].bitcast(BF16) for t in range(NT)]
        z_ring = [A4[:, 0:2048], A4[:, 2048:4096]]
        z2_ring = [A6[:, 0:2048], A6[:, 2048:4096]]
        ztoks = []
        zsp = []
        prevz = [(), ()]
        for t in range(NT):
            sl = z_ring[t % 2]
            sl2 = z2_ring[t % 2]
            x1l = P.dma("G", sl, h["x1_hbm"][t],
                        deps=(it2 if t < 2 else []) + list(prevz[t % 2]),
                        tag=f"zx{t}")
            fl_ = P.dma("G", sl2, h["res_hbm"][t],
                        deps=it_ret_by_tile[t], tag=f"zf{t}")
            za = P.op("V", lambda v, sl=sl, sl2=sl2: v.tensor_add(sl, sl, sl2),
                      deps=[x1l, fl_], tag=f"za{t}")
            zc = P.op("V", lambda v, t=t, sl=sl: v.tensor_copy(zb[t], sl),
                      deps=[za], tag=f"zc{t}")
            zq = P.op("V", lambda v, t=t: v.tensor_mul(zsq[t], zb[t], zb[t]),
                      deps=[zc], tag=f"zq{t}")
            sp = P.dma("G", h["x1_hbm"][t], sl, deps=[za], tag=f"zs{t}")
            zsp.append(sp)
            prevz[t % 2] = (za, zc, zq, sp)
            ztoks.extend([zq])
        stok = ln_stats(zb, zsq, ztoks)
        bt1, bt2, BC2 = ln_rows(stok, "f")
        prevs = [tuple(prevz[0]), tuple(prevz[1])]
        for t in range(NT):
            sl = z_ring[t % 2]
            xl = P.dma("G", sl, h["x1_hbm"][t],
                       deps=[bt1, zsp[t]] + list(prevs[t % 2]), tag=f"for{t}")
            m1 = P.op("V", lambda v, sl=sl: v.tensor_mul(sl, sl, BC[:, :]),
                      deps=[xl, bt1], tag=f"fom{t}")
            m2 = P.op("V", lambda v, sl=sl: v.tensor_add(sl, sl, BC2),
                      deps=[m1, bt2], tag=f"foa{t}")
            od = P.dma("G", h["outT"][t * 128:(t + 1) * 128, :], sl,
                       deps=[m2], tag=f"foo{t}")
            prevs[t % 2] = (od,)
        P.join(engs=("G",))
        P.emit(blk)
    return nc



def kernel(**inputs):
    sh, xts = _host_prep(inputs)
    nc = build()
    in_maps = []
    for c in range(8):
        m = dict(sh)
        m["xT"] = xts[c]
        in_maps.append(m)
    res = run_bass_kernel_spmd(nc, in_maps, core_ids=list(range(8)))
    outs = [np.asarray(r["outT"]).T for r in res.results]
    return np.ascontiguousarray(np.stack(outs)).astype(np.float32)


# version 25 (final)
# speedup vs baseline: 1.4273x; 1.0117x over previous
import contextlib
import numpy as np
import ml_dtypes
import concourse.bass as bass
import concourse.mybir as mybir
from concourse.bass_utils import run_bass_kernel_spmd

F32 = mybir.dt.float32
BF16 = mybir.dt.bfloat16
AF = mybir.ActivationFunctionType
ALU = mybir.AluOpType
AXX = mybir.AxisListType.X

S = 2048
D = 1024
NT = D // 128
HD = 64
H16 = 16
H8 = 8
HID = 2730
HIDP = 2816
NTH = HIDP // 128
WIN = 512
LAM = 0.01
BF = ml_dtypes.bfloat16
SQ2I = 0.7071067811865476



def _tile_lhsT(w):
    din, dout = w.shape
    nc_, nt_ = din // 128, dout // 128
    return np.ascontiguousarray(w.reshape(nc_, 128, nt_, 128).transpose(2, 0, 1, 3))


def _host_consts():
    c = {}
    p = np.arange(128)[:, None]
    j = np.arange(4096)[None, :]
    c["decay"] = (np.exp(-LAM * np.abs(p - j + 2048)) / 32.0).astype(np.float32)
    cc = np.arange(2048)[None, :] * np.ones((128, 1))
    c["edec"] = np.stack([np.exp(LAM * cc), np.exp(-LAM * cc) / 32.0]
                         ).astype(np.float32)
    c["ident"] = np.eye(128, dtype=BF)
    c["ones_bf"] = np.ones((128, 1), dtype=BF)
    half = np.arange(0, HD, 2, dtype=np.float64)
    inv_freq = 1.0 / 10000.0 ** (half / HD)
    t = np.arange(S, dtype=np.float64)
    freqs = np.concatenate([t[:, None] * inv_freq[None, :]] * 2, axis=1)
    base = (half + 0.4 * HD) / (1.4 * HD)
    power = (t - S // 2) / float(WIN // 2)
    scale = np.concatenate([base[None, :] ** power[:, None]] * 2, axis=1)
    cos, sin = np.cos(freqs), np.sin(freqs)
    sgn = np.ones((S, HD))
    sgn[:, 0:HD // 2] = -1.0
    for name, m in [("rot_qc1", 8.0 * cos * scale), ("rot_qc2", 8.0 * sin * scale * sgn),
                    ("rot_kc3", cos / scale), ("rot_kc4", sin / scale * sgn)]:
        mt = m.T
        c[name] = np.ascontiguousarray(
            np.concatenate([mt, mt], axis=0).astype(np.float32))
    masks = np.zeros((5, 128, 640), np.float32)
    cc = np.arange(640)[None, :]
    pp = np.arange(128)[:, None]
    NEG = np.float32(-3.0e38)
    for qt in range(4):
        masks[qt] = np.where(cc <= (qt * 128 + pp), 0.0, NEG)
    masks[4] = np.where((cc >= pp) & (cc <= pp + WIN), 0.0, NEG)
    c["lmask"] = masks
    return c


def _host_prep(inputs):
    sh = _host_consts()
    sh["wq"] = _tile_lhsT(np.asarray(inputs["L0_attn_W_q"]))
    sh["wk"] = _tile_lhsT(np.asarray(inputs["L0_attn_W_k"]))
    sh["wv"] = _tile_lhsT(np.asarray(inputs["L0_attn_W_v"]))
    sh["wdown"] = _tile_lhsT(np.asarray(inputs["L0_attn_W_down_kv"]))
    sh["wupk"] = _tile_lhsT(np.asarray(inputs["L0_attn_W_up_k"]))
    sh["wupv"] = np.ascontiguousarray(
        np.asarray(inputs["L0_attn_W_up_v"]).reshape(2, 128, D))
    sh["wo"] = _tile_lhsT(np.asarray(inputs["L0_attn_W_o"]))

    wqkv = np.asarray(inputs["L0_ffn_wqkv"])
    sh["fwqk"] = np.stack([_tile_lhsT(wqkv[l][:, 0:1024]) for l in range(2)])
    sh["fwv"] = np.ascontiguousarray(wqkv[:, :, 1024:1536].reshape(2, NT, 128, 512))
    fwo = np.asarray(inputs["L0_ffn_wo"])
    sh["fwo"] = np.stack([_tile_lhsT(fwo[l]) for l in range(2)])
    w1 = np.asarray(inputs["L0_ffn_w1"])
    w1a = np.zeros((2, D, HIDP), np.float32)
    w1g = np.zeros((2, D, HIDP), np.float32)
    w1a[:, :, 0:HID] = w1[:, :, 0:HID]
    w1g[:, :, 0:HID] = w1[:, :, HID:2 * HID]
    sh["fw1a"] = np.stack([_tile_lhsT(w1a[l]) for l in range(2)])
    sh["fw1g"] = np.stack([_tile_lhsT(w1g[l]) for l in range(2)])
    w2 = np.asarray(inputs["L0_ffn_w2"])
    w2p = np.zeros((2, HIDP, D), np.float32)
    w2p[:, 0:HID, :] = w2
    sh["fw2"] = np.stack([_tile_lhsT(w2p[l]) for l in range(2)])
    b1 = np.asarray(inputs["L0_ffn_b1"])
    b1a = np.zeros((2, HIDP), np.float32)
    b1g = np.zeros((2, HIDP), np.float32)
    b1a[:, 0:HID] = b1[:, 0:HID]
    b1g[:, 0:HID] = b1[:, HID:2 * HID]
    sh["fb1a"] = np.ascontiguousarray(b1a.reshape(2, NTH, 128).transpose(0, 2, 1))
    sh["fb1g"] = np.ascontiguousarray(b1g.reshape(2, NTH, 128).transpose(0, 2, 1))
    sh["fb2"] = np.ascontiguousarray(
        np.asarray(inputs["L0_ffn_b2"]).reshape(2, NT, 128).transpose(0, 2, 1))
    sh["flng"] = np.ascontiguousarray(
        np.asarray(inputs["L0_ffn_ln_g"]).reshape(2, NT, 128).transpose(0, 2, 1))
    sh["flnb"] = np.ascontiguousarray(
        np.asarray(inputs["L0_ffn_ln_b"]).reshape(2, NT, 128).transpose(0, 2, 1))
    sh["fqs"] = np.ascontiguousarray(
        np.tile(np.asarray(inputs["L0_ffn_q_scale"]), (1, 2))[:, :, None])
    sh["fks"] = np.ascontiguousarray(
        np.tile(np.asarray(inputs["L0_ffn_k_scale"]), (1, 2))[:, :, None])
    x = np.asarray(inputs["L0_attn_x"])
    xts = [np.ascontiguousarray(x[i].T) for i in range(x.shape[0])]
    return sh, xts



ENGS = ("Y", "G", "T", "V", "S")


class Prog:

    NPOOL = 10

    def __init__(self, sems, pools):
        self.sems = sems
        self.pools = pools
        self.q = {e: [] for e in ENGS}
        self.cnt = {k: 0 for k in sems}
        self.pool_i = {e: 0 for e in ("Y", "G")}
        self.lastw = {}

    def _wait(self, eng, deps):
        w = []
        for d in deps:
            if d is None:
                continue
            key, val = d
            k = (eng, key)
            if self.lastw.get(k, 0) >= val:
                continue
            self.lastw[k] = val
            w.append((key, val))
        return w

    def op(self, eng, fn, deps=(), inc=1, tag=""):
        w = self._wait(eng, deps)
        self.q[eng].append((w, fn, eng if inc else None, inc, tag))
        if inc:
            self.cnt[eng] += inc
        return (eng, self.cnt[eng])

    def dma(self, eng, out_ap, in_ap, deps=(), tag=""):
        key = self.pools[eng][self.pool_i[eng] % self.NPOOL]
        self.pool_i[eng] += 1
        gate = (key, self.cnt[key]) if self.cnt[key] else None
        w = self._wait(eng, list(deps) + ([gate] if gate else []))
        fn = lambda e, o=out_ap, i=in_ap: e.dma_start(o, i)
        self.q[eng].append((w, fn, key, 16, tag))
        self.cnt[key] += 16
        return (key, self.cnt[key])

    def tok(self, eng):
        return (eng, self.cnt[eng])

    def dtok(self, eng):
        return [(k, self.cnt[k]) for k in self.pools[eng] if self.cnt[k]]

    def join(self, engs=ENGS):
        toks = [(e, self.cnt[e]) for e in ("T", "V", "S")] +             self.dtok("Y") + self.dtok("G")
        out = []
        for e in engs:
            if e == "T":
                out.append(self.op("T", lambda t: t.nop(), deps=toks, inc=1,
                                   tag="joinT"))
            elif e == "V":
                out.append(self.op("V", lambda v: v.engine_nop(), deps=toks,
                                   inc=1, tag="joinV"))
            elif e == "S":
                out.append(self.op("S", lambda s: s.nop(), deps=toks,
                                   inc=1, tag="joinS"))
            elif e in ("Y", "G"):
                out.append(self.op(e, lambda q: q.nop(), deps=toks, inc=1,
                                   tag="join" + e))
        return out

    def emit(self, blk):
        sems = self.sems

        def body(engname):
            def _(eng):
                for w, fn, skey, inc, tag in self.q[engname]:
                    for key, val in w:
                        eng.wait_ge(sems[key], val)
                    inst = fn(eng)
                    if inc:
                        assert inst is not None, f"inc op returned None: {tag}"
                        inst.then_inc(sems[skey], inc)
            return _
        blk.sync(body("Y"))
        blk.gpsimd(body("G"))
        blk.tensor(body("T"))
        blk.vector(body("V"))
        blk.scalar(body("S"))



def build(upto="full", dump=(), joins=()):
    nc = bass.Bass("TRN2", target_bir_lowering=False)
    dump = set(dump)

    h = {}
    h["xT"] = nc.dram_tensor("xT", [D, S], F32, kind="ExternalInput").ap()
    shapes = dict(
        wq=[NT, NT, 128, 128], wk=[NT, NT, 128, 128], wv=[NT, NT, 128, 128],
        wdown=[2, NT, 128, 128], wupk=[NT, 2, 128, 128], wupv=[2, 128, D],
        wo=[NT, NT, 128, 128],
        fwqk=[2, NT, NT, 128, 128], fwv=[2, NT, 128, 512],
        fwo=[2, NT, 4, 128, 128],
        fw1a=[2, NTH, NT, 128, 128], fw1g=[2, NTH, NT, 128, 128],
        fw2=[2, NT, NTH, 128, 128],
        fb1a=[2, 128, NTH], fb1g=[2, 128, NTH], fb2=[2, 128, NT],
        flng=[2, 128, NT], flnb=[2, 128, NT], fqs=[2, 128, 1], fks=[2, 128, 1],
        decay=[128, 4096], edec=[2, 128, 2048],
        rot_qc1=[128, S], rot_qc2=[128, S], rot_kc3=[128, S], rot_kc4=[128, S],
        lmask=[5, 128, 640],
    )
    for k, shp in shapes.items():
        h[k] = nc.dram_tensor(k, shp, F32, kind="ExternalInput").ap()
    h["ident"] = nc.dram_tensor("ident", [128, 128], BF16, kind="ExternalInput").ap()
    h["ones_bf"] = nc.dram_tensor("ones_bf", [128, 1], BF16, kind="ExternalInput").ap()
    h["outT"] = nc.dram_tensor("outT", [D, S], F32, kind="ExternalOutput").ap()
    h["x1_hbm"] = nc.dram_tensor("x1_hbm", [NT, 128, S], F32, kind="Internal").ap()
    h["res_hbm"] = nc.dram_tensor("res_hbm", [NT, 128, S], F32, kind="Internal").ap()
    h["qT_sp"] = nc.dram_tensor("qT_sp", [NT, 128, S], BF16, kind="Internal").ap()
    h["kr_sp"] = nc.dram_tensor("kr_sp", [NT, 128, S], BF16, kind="Internal").ap()
    h["rows_hbm"] = nc.dram_tensor("rows_hbm", [8, S], F32, kind="Internal").ap()
    dumps = {name: nc.dram_tensor("dbg_" + name, [D, S], F32,
                                  kind="ExternalOutput").ap() for name in dump}

    st = contextlib.ExitStack()
    with st:
        def sb(name, cols, dtype=F32):
            tns = st.enter_context(nc.sbuf_tensor(name, [128, cols], dtype))
            return tns[:]

        resb = sb("resb", 16384, BF16)
        A1 = sb("A1", 8320)
        A2 = sb("A2", 8192)
        A3 = sb("A3", 4096)
        A4 = sb("A4", 4096)
        A5 = sb("A5", 4096)
        A6 = sb("A6", 4096)
        WS = sb("WS", 3072)
        RW = sb("RW", 2048)
        BC = sb("BC", 2048)
        CST = sb("CST", 512)

        psA = st.enter_context(nc.psum_tensor("psA", [128, 2048], F32))[:]
        psB = st.enter_context(nc.psum_tensor("psB", [128, 2048], F32))[:]

        sems = {}
        pools = {}
        for e in ("T", "V", "S", "Y", "G"):
            sems[e] = st.enter_context(nc.semaphore("sem" + e))
        for e in ("Y", "G"):
            pools[e] = []
            for i in range(Prog.NPOOL):
                key = e + str(i)
                sems[key] = st.enter_context(nc.semaphore("sem" + key))
                pools[e].append(key)
        blk = st.enter_context(nc.Block())
        P = Prog(sems, pools)

        ident_bf = CST[:, 0:64].bitcast(BF16)
        ones_bf = CST[:, 64:65].bitcast(BF16)
        qs_col = CST[:, 66:67]
        ks_col = CST[:, 67:68]
        b1a_cols = CST[:, 96:96 + NTH]
        b1g_cols = CST[:, 128:128 + NTH]
        b1gs_cols = CST[:, 160:160 + NTH]
        b2_cols = CST[:, 192:192 + NT]
        lng_cols = CST[:, 208:208 + NT]
        lnb_cols = CST[:, 224:224 + NT]
        eps_col = CST[:, 240:241]
        nm_cols = [CST[:, 244 + i:245 + i] for i in range(2)]
        rc_cols = [CST[:, 248 + i:249 + i] for i in range(2)]
        rowA = RW[:, 0:2048]

        c_tok = [P.dma("Y", ident_bf[:, 0:128], h["ident"][:], tag="ident"),
                 P.dma("Y", ones_bf[:, 0:1], h["ones_bf"][:], tag="ones"),
                 P.op("V", lambda v: v.memset(eps_col[0:1, 0:1], 1e-5), tag="eps")]
        resb_t = [resb[:, t * S:(t + 1) * S] for t in range(NT)]

        ws_state = [(), ()]

        def wstream(hbm_blocks, n_c, slot, deps=()):
            stage = WS[:, slot * 1536: slot * 1536 + n_c * 128]
            wbuf = WS[:, slot * 1536 + 1024: slot * 1536 + 1024 + n_c * 64].bitcast(BF16)
            tok = P.dma("Y", stage.rearrange("p (c f) -> p c f", f=128),
                        hbm_blocks.rearrange("c p f -> p c f"),
                        deps=list(deps) + list(ws_state[slot]), tag="wload")
            ctok = P.op("V", lambda v, o=wbuf, i=stage: v.tensor_copy(o, i),
                        deps=[tok], tag="wcast")
            return wbuf, ctok

        def mm_group(ps_ap, lhs_fn, rhs_fn, n_c, deps, tag, n_nc=4, ncw=512,
                     start_c=0, stop_c=None, tp=None):
            stop_c = n_c - 1 if stop_c is None else stop_c
            tok = None
            for ncx in range(n_nc):
                for ci in range(n_c):
                    def mm(T, ci=ci, ncx=ncx):
                        return T.matmul(ps_ap[:, ncx * ncw:(ncx + 1) * ncw],
                                        lhs_fn(ci), rhs_fn(ci, ncx),
                                        start=(ci == start_c), stop=(ci == stop_c),
                                        tile_position=tp)
                    tok = P.op("T", mm, deps=deps if (ncx == 0 and ci == 0) else (),
                               inc=(1 if (ncx == n_nc - 1 and ci == n_c - 1) else 0),
                               tag=tag)
            return tok

        bcast_slot_cons = {}

        def bcast(src_row_ap, dst_ap, nrows, hbm_slot, deps, tag="bc"):
            r = h["rows_hbm"][hbm_slot:hbm_slot + 1, :]
            t1 = P.dma("G", r, src_row_ap,
                       deps=list(deps) + [bcast_slot_cons.get(hbm_slot)],
                       tag=tag + "o")
            t2 = P.dma("G", dst_ap, r.broadcast_to([nrows, S]), deps=[t1],
                       tag=tag + "i")
            bcast_slot_cons[hbm_slot] = t2
            return t2

        def dump_tile(name, t, ap, deps):
            if name in dumps:
                P.dma("G", dumps[name][t * 128:(t + 1) * 128, :], ap, deps=deps,
                      tag="dump")

        def ln_stats(bf_tiles, sq_tiles, deps):
            last = None
            for which, rhs in ((0, bf_tiles), (32, sq_tiles)):
                for ncx in range(4):
                    for c in range(NT):
                        def mm(T, which=which, ncx=ncx, c=c, rhs=rhs):
                            return T.matmul(
                                psB[which:which + 1, ncx * 512:(ncx + 1) * 512],
                                ones_bf[:, 0:1],
                                rhs[c][:, ncx * 512:(ncx + 1) * 512],
                                start=(c == 0), stop=(c == NT - 1),
                                tile_position=(0, which))
                        last = P.op("T", mm,
                                    deps=deps if (which == 0 and ncx == 0 and c == 0) else (),
                                    inc=(1 if (c == NT - 1) else 0), tag="lnstat")
            return last

        def ln_rows(stok, tag):
            R0 = rowA[0:1, :]
            R1 = BC[0:1, :]
            R2 = A5[0:1, 0:2048]
            a1 = P.op("V", lambda v: v.tensor_scalar_mul(R0, psB[0:1, :], 1.0 / D),
                      deps=[stok], tag=tag + "_mu")
            c0 = P.op("V", lambda v: v.tensor_copy(rowA[32:33, :], psB[32:33, :]),
                      deps=[stok], tag=tag + "_s2ev")
            d0 = P.dma("Y", R1, rowA[32:33, :], deps=[c0], tag=tag + "_s2sh")
            a2 = P.op("V", lambda v: v.tensor_scalar_mul(R1, R1, 1.0 / D),
                      deps=[d0], tag=tag + "_m2")
            a3 = P.op("V", lambda v: v.tensor_mul(R2, R0, R0),
                      deps=[a1], tag=tag + "_musq")
            a4 = P.op("V", lambda v: v.tensor_sub(R1, R1, R2),
                      deps=[a2, a3], tag=tag + "_var")
            a6 = P.op("S", lambda s: s.activation(R1, R1, AF.Ln,
                                                  bias=eps_col[0:1, 0:1]),
                      deps=[a4], tag=tag + "_ln")
            a7 = P.op("S", lambda s: s.activation(R1, R1, AF.Exp, scale=-0.5),
                      deps=[a6], tag=tag + "_exp")
            a8 = P.op("V", lambda v: v.scalar_tensor_tensor(
                R2, R0, -1.0, R1, ALU.mult, ALU.mult), deps=[a7],
                tag=tag + "_c1")
            bt1 = bcast(R1, BC[:, :], 128, 4, deps=[a7, a8], tag=tag + "b1")
            BC2 = A5[:, 0:2048]
            bt2 = bcast(R2, BC2, 128, 5, deps=[a8], tag=tag + "b2")
            return bt1, bt2, BC2

        x_ring = [A4[:, 0:2048], A4[:, 2048:4096]]
        xc = []
        for t in range(NT):
            sl = x_ring[t % 2]
            dt_ = P.dma("G", sl, h["xT"][t * 128:(t + 1) * 128, :],
                        deps=[xc[t - 2]] if t >= 2 else (), tag="xload")
            ct = P.op("V", lambda v, o=resb_t[t], i=sl: v.tensor_copy(o, i),
                      deps=[dt_], tag="xcast")
            xc.append(ct)
        xb_ready = [P.tok("V")] + c_tok

        kvT = [A2[:, t * 1024:(t + 1) * 1024].bitcast(BF16) for t in range(NT)]
        spl = [A5[:, 0:1024].bitcast(BF16), A5[:, 1024:2048].bitcast(BF16)]
        spl_tok = [(), ()]

        def spill_evac(dst_hbm):
            def f(t, ps, tok):
                b = spl[t % 2]
                e = P.op("V", lambda v, b=b, ps=ps: v.tensor_copy(b, ps),
                         deps=[tok] + list(spl_tok[t % 2]), tag="spev")
                s_ = P.dma("G", dst_hbm[t], b, deps=[e], tag="spill")
                spl_tok[t % 2] = (s_,)
                return [e]
            return f

        def proj(out_cb, w_hbm, n_c, rhs_tiles, n_t, deps=(), tag="proj"):
            readers = [(), ()]
            for t in range(n_t):
                slot = t % 2
                ps = psA if slot == 0 else psB
                wbuf, wtok = wstream(w_hbm[t], n_c, slot, deps=deps)
                dd = [wtok] + list(deps) + list(readers[slot])
                tok = mm_group(ps, lambda c, wbuf=wbuf: wbuf[:, c * 128:(c + 1) * 128],
                               lambda c, ncx: rhs_tiles[c][:, ncx * 512:(ncx + 1) * 512],
                               n_c, dd, f"{tag}{t}")
                readers[slot] = out_cb(t, ps, tok)
                ws_state[slot] = (tok,)
            return readers

        proj(spill_evac(h["qT_sp"]), h["wq"], NT, resb_t, NT,
             deps=xb_ready, tag="q")

        def kv_proj():
            readers = [(), ()]
            for t in range(NT):
                slot = t % 2
                ps = psA if slot == 0 else psB
                stage1 = WS[:, slot * 1536: slot * 1536 + 1024]
                stage2 = A5[:, 2048 + slot * 1024: 3072 + slot * 1024]
                wbuf = WS[:, slot * 1536 + 1024: slot * 1536 + 1536].bitcast(BF16)
                tok1 = P.dma("Y", stage1.rearrange("p (c f) -> p c f", f=128),
                             h["wk"][t].rearrange("c p f -> p c f"),
                             deps=list(ws_state[slot]), tag="wk")
                tok2 = P.dma("Y", stage2.rearrange("p (c f) -> p c f", f=128),
                             h["wv"][t].rearrange("c p f -> p c f"),
                             tag="wv")
                ctok = P.op("V", lambda v, o=wbuf, a=stage1, b=stage2:
                            v.tensor_add(o, a, b), deps=[tok1, tok2], tag="kvadd")
                dd = [ctok] + list(readers[slot])
                tok = mm_group(ps, lambda c, wbuf=wbuf: wbuf[:, c * 128:(c + 1) * 128],
                               lambda c, ncx: resb_t[c][:, ncx * 512:(ncx + 1) * 512],
                               NT, dd, f"kv{t}")
                e = P.op("V", lambda v, o=kvT[t], i=ps: v.tensor_copy(o, i),
                         deps=[tok], tag="kvev")
                readers[slot] = [e]
                ws_state[slot] = (tok,)
        kv_proj()

        latb = [A5[:, 2048 + i * 1024: 3072 + i * 1024].bitcast(BF16)
                for i in range(2)]
        lat_deps = [P.tok("V")] + P.dtok("Y")

        def lat_evac(t, ps, tok):
            e = P.op("V", lambda v, o=latb[t], i=ps: v.tensor_copy(o, i),
                     deps=[tok], tag="latev")
            return [e]
        proj(lat_evac, h["wdown"], NT, kvT, 2, deps=lat_deps, tag="lat")
        lat_ready = [P.tok("V")]

        proj(spill_evac(h["kr_sp"]), h["wupk"], 2, latb, NT,
             deps=lat_ready, tag="kr")

        vext = A1[:, 0:8320].bitcast(BF16)
        vv4 = vext.rearrange("p (t h c) -> p t h c", t=16, h=16, c=65)
        ones_tok = P.op("V", lambda v: v.memset(vv4[:, :, :, 64:65], 1.0),
                        deps=lat_ready, tag="vext1")
        wupv_stage = A4[:, 0:2048]
        wupv_buf = A4[:, 2048:3072].bitcast(BF16)
        lv = P.dma("Y", wupv_stage.rearrange("p (c f) -> p c f", f=D),
                   h["wupv"].rearrange("c p f -> p c f"),
                   deps=lat_ready + [xc[-1]], tag="wupv")
        lvc = P.op("V", lambda v: v.tensor_copy(wupv_buf, wupv_stage),
                   deps=[lv], tag="wupvc")
        vrd = [(), ()]
        for tt in range(16):
            slot = tt % 2
            ps = (psA if slot == 0 else psB)[:, 0:1024]
            dd = [lvc, ones_tok] + list(vrd[slot])
            tok = None
            for nn in range(2):
                for c in range(2):
                    def mm(T, tt=tt, c=c, nn=nn, ps=ps):
                        return T.matmul(
                            ps[:, nn * 512:(nn + 1) * 512],
                            latb[c][:, tt * 128:(tt + 1) * 128],
                            wupv_buf[:, c * 1024 + nn * 512:
                                     c * 1024 + (nn + 1) * 512],
                            start=(c == 0), stop=(c == 1))
                    tok = P.op("T", mm, deps=dd if (nn == 0 and c == 0) else (),
                               inc=(1 if (nn == 1 and c == 1) else 0),
                               tag=f"vr{tt}")
            e = P.op("V", lambda v, tt=tt, ps=ps:
                     v.tensor_copy(vv4[:, tt, :, 0:64],
                                   ps.rearrange("p (hh c) -> p hh c", c=64)),
                     deps=[tok], tag=f"vrev{tt}")
            vrd[slot] = (e,)
        vext_ready = [P.tok("V")]

        decay_sb = A3[:, 0:4096]
        dtok = P.dma("Y", decay_sb, h["decay"][:], deps=lat_ready, tag="decay")

        qbuf = [A2[:, 0:1024].bitcast(BF16), A2[:, 1024:2048].bitcast(BF16)]
        krbuf = [A2[:, 2048:3072].bitcast(BF16), A2[:, 3072:4096].bitcast(BF16)]
        smask_ring = [A5[:, 0:2048], A5[:, 2048:4096]]
        expS_ring = [resb[:, i * 2048:(i + 1) * 2048] for i in range(3)]
        uraw = resb[:, 6144:10240].bitcast(F32)
        attnout = [A6[:, t * 1024:(t + 1) * 1024].bitcast(BF16) for t in range(4)] + \
                  [A2[:, 4096 + t * 1024: 5120 + t * 1024].bitcast(BF16)
                   for t in range(4)]

        pair_cons = [(), ()]
        ascr = resb[:, 10240:12288]
        prev_an = ()
        prev_anops = ()
        prev_bt = None
        prev_ash = None
        ed_p = A4[:, 0:2048]
        ed_m = A4[:, 2048:4096]
        edtok = [P.dma("Y", ed_p, h["edec"][0], deps=vext_ready, tag="edp"),
                 P.dma("Y", ed_m, h["edec"][1], deps=vext_ready, tag="edm")]
        qpb = [resb[:, 12288:14336]] * 2
        qmb = [resb[:, 14336:16384]] * 2
        kpb = [A5[:, i * 1024:(i + 1) * 1024].bitcast(BF16) for i in range(2)]
        kmb = [A5[:, 2048 + i * 1024:3072 + i * 1024].bitcast(BF16)
               for i in range(2)]
        scP4 = [psA[:, i * 512:(i + 1) * 512] for i in range(4)]
        expS8 = [resb[:, i * 512:(i + 1) * 512] for i in range(8)]
        uraw = resb[:, 6144:10240].bitcast(F32)
        sc_cons = {}
        es_cons = {}
        slot_vd = {}
        prep_cons = ()
        prev_uevac = None
        pendq = []
        self_last = [None]
        pidx = 0
        lds = {}

        def emit_ld(pt):
            slot = pt % 2
            l1 = P.dma("Y", qbuf[slot], h["qT_sp"][pt],
                       deps=list(pair_cons[slot]) + vext_ready + [dtok] +
                       edtok + list(spl_tok[0]) + list(spl_tok[1]),
                       tag=f"qld{pt}")
            l2 = P.dma("Y", krbuf[slot], h["kr_sp"][pt], tag=f"krld{pt}")
            lds[pt] = (l1, l2)

        emit_ld(0)
        for pt in range(8):
            slot = pt % 2
            if pt + 1 < 8:
                emit_ld(pt + 1)
            ld1, ld2 = lds[pt]
            u_lasts = []
            pv = []
            for dst, src_, e_ in ((qpb[slot], qbuf[slot], ed_p),
                                  (qmb[slot], qbuf[slot], ed_m),
                                  (kpb[slot], krbuf[slot], ed_p),
                                  (kmb[slot], krbuf[slot], ed_m)):
                pv.append(P.op("V", lambda v, dst=dst, src_=src_, e_=e_:
                               v.tensor_mul(dst, src_, e_),
                               deps=[ld1, ld2] + list(prep_cons),
                               tag=f"prep{pt}"))
            head_last = []
            for qh in range(2):
                for kc in range(16):
                    band_ci = kc // 4
                    brel = band_ci - 2 * qh
                    qq_seq = (1, 0) if brel == 0 else (0, 1)
                    etoks = {0: [], 1: []}
                    for qq_rel in qq_seq:
                        qq = 2 * qh + qq_rel
                        for hh in range(2):
                            hb = hh * 64
                            habs = 2 * pt + hh
                            sl4 = 2 * hh + qq_rel
                            sps = scP4[sl4]
                            dep0 = ([pv[-1]] +
                                    list(slot_vd.pop(sl4, ())) +
                                    list(sc_cons.get(sl4, ())))
                            vd = None
                            if qq == band_ci:
                                dloc = (kc % 4) * 128
                                def mmd(T, hb=hb, kc=kc, qq=qq, sps=sps,
                                        dloc=dloc, slot=slot):
                                    return T.matmul(
                                        sps[:, dloc:dloc + 128],
                                        krbuf[slot][hb:hb + 64,
                                                    kc * 128:(kc + 1) * 128],
                                        qbuf[slot][hb:hb + 64,
                                                   qq * 512 + dloc:
                                                   qq * 512 + dloc + 128],
                                        start=True, stop=True,
                                        tile_position=(hb, 0))
                                stok = P.op("T", mmd, deps=dep0, inc=1,
                                            tag=f"sc{habs}_{kc}_{qq}d")
                                for c0, c1, lh, rh in (
                                        (0, dloc, kmb[slot], qpb[slot]),
                                        (dloc + 128, 512,
                                         kpb[slot], qmb[slot])):
                                    if c1 <= c0:
                                        continue
                                    def mms(T, hb=hb, kc=kc, qq=qq, sps=sps,
                                            c0=c0, c1=c1, lh=lh, rh=rh):
                                        return T.matmul(
                                            sps[:, c0:c1],
                                            lh[hb:hb + 64,
                                               kc * 128:(kc + 1) * 128],
                                            rh[hb:hb + 64,
                                               qq * 512 + c0:qq * 512 + c1],
                                            start=True, stop=True,
                                            tile_position=(hb, 0))
                                    stok = P.op("T", mms, deps=(), inc=1,
                                                tag=f"sc{habs}_{kc}_{qq}s")
                                vd = P.op("V", lambda v, sps=sps, dloc=dloc:
                                          v.tensor_mul(
                                              sps[:, dloc:dloc + 128],
                                              sps[:, dloc:dloc + 128],
                                              decay_sb[:, 2048:2176]),
                                          deps=[stok], tag=f"bd{habs}_{kc}")
                                slot_vd[sl4] = (vd,)
                            else:
                                if qq > band_ci:
                                    lhs_t, rhs_t = kpb[slot], qmb[slot]
                                else:
                                    lhs_t, rhs_t = kmb[slot], qpb[slot]
                                def mm(T, hb=hb, kc=kc, qq=qq, lhs_t=lhs_t,
                                       rhs_t=rhs_t, sps=sps):
                                    return T.matmul(
                                        sps,
                                        lhs_t[hb:hb + 64,
                                              kc * 128:(kc + 1) * 128],
                                        rhs_t[hb:hb + 64,
                                              qq * 512:(qq + 1) * 512],
                                        start=True, stop=True,
                                        tile_position=(hb, 0))
                                stok = P.op("T", mm, deps=dep0,
                                            inc=1, tag=f"sc{habs}_{kc}_{qq}")
                            etoks[hh] += [stok, vd]
                            if qq_rel == qq_seq[1]:
                                habs_h = 2 * pt + hh
                                e0 = 2 * (pidx % 4)
                                dd_e = (etoks[hh] +
                                        [es_cons.get(e0),
                                         es_cons.get(e0 + 1)])
                                er_pair = resb[:, e0 * 512:e0 * 512 + 1024]
                                se = P.op(
                                    "S",
                                    lambda s, hh=hh, er_pair=er_pair:
                                    s.activation(er_pair,
                                                 psA[:, hh * 1024:
                                                     hh * 1024 + 1024],
                                                 AF.Exp),
                                    deps=dd_e, tag=f"ex{habs_h}_{kc}")
                                sc_cons[2 * hh] = (se,)
                                sc_cons[2 * hh + 1] = (se,)
                                if len(pendq) >= 2:
                                    pendq.pop(0)()

                                def uemit(kc=kc, habs=habs_h, hh=hh, se=se,
                                          pu=prev_uevac, qh=qh, e0=e0):
                                    for j in range(2):
                                        erj = expS8[e0 + j]
                                        bank = 2 * hh + j
                                        def mmU(T, kc=kc, habs=habs,
                                                bank=bank, erj=erj):
                                            return T.matmul(
                                                psB[0:65, bank * 512:
                                                    (bank + 1) * 512],
                                                vext[:, kc * 1040 + habs * 65:
                                                     kc * 1040 + habs * 65
                                                     + 65],
                                                erj,
                                                start=(kc == 0),
                                                stop=(kc == 15))
                                        tok = P.op(
                                            "T", mmU,
                                            deps=[se] + ([pu] if (kc == 0 and
                                                                  j == 0)
                                                         else []),
                                            inc=1, tag=f"U{habs}_{kc}_{j}")
                                        es_cons[e0 + j] = tok
                                        self_last[0] = tok
                                pendq.append(uemit)
                                pidx += 1
                while pendq:
                    pendq.pop(0)()
                u_last = self_last[0]
                ue = P.op("S", lambda s: s.copy(uraw[0:65, :], psB[0:65, :]),
                          deps=[u_last] + list(prev_anops),
                          tag=f"uev{pt}_{qh}")
                prev_uevac = ue
                sln = P.op("S", lambda s: s.activation(rowA[64:65, :],
                                                       uraw[64:65, :],
                                                       AF.Ln),
                           deps=[ue, prev_bt], tag=f"uln{pt}_{qh}")
                sex = P.op("S", lambda s: s.activation(rowA[64:65, :],
                                                       rowA[64:65, :],
                                                       AF.Exp, scale=-1.0),
                           deps=[sln], tag=f"uex{pt}_{qh}")
                bt = bcast(rowA[64:65, :], BC[0:64, :], 64, (2 * pt + qh) % 4,
                           deps=[sex] + list(prev_an), tag=f"ub{pt}_{qh}")
                prev_bt = bt
                qc0 = qh * 1024
                an_h0 = P.op("G", lambda g, pt=pt, qc0=qc0:
                             g.tensor_mul(
                                 attnout[pt][0:64, qc0:qc0 + 1024],
                                 uraw[0:64, 0:1024], BC[0:64, 0:1024]),
                             deps=[bt, ue], tag=f"an{pt}_{qh}")
                an0 = P.op("G", lambda g:
                           g.tensor_mul(ascr[0:64, 0:1024],
                                        uraw[0:64, 1024:2048],
                                        BC[0:64, 1024:2048]),
                           deps=[bt, ue, prev_ash], tag=f"an1{pt}_{qh}")
                an = P.dma("G", attnout[pt][64:128, qc0:qc0 + 1024],
                           ascr[0:64, 0:1024],
                           deps=[an0], tag=f"ash{pt}_{qh}")
                prev_ash = an
                prev_an = (an_h0, an)
                prev_anops = (an_h0, an0)
                head_last += [an_h0, an, u_last]
                u_lasts.append(u_last)
            pair_cons[slot] = tuple(pv) + tuple(u_lasts)
            prep_cons = tuple(u_lasts)
        att_done = [P.tok("V"), P.tok("S"), P.tok("T")] + P.dtok("G")

        sq_t = [A1[:, t * 1024:(t + 1) * 1024].bitcast(BF16) for t in range(NT)]
        y_ring = [A5[:, 0:2048], A5[:, 2048:4096]]
        y_sp = []
        wo_ring_cons = [tuple(att_done), tuple(att_done)]

        def wo_evac(t, ps, tok):
            sl = y_ring[t % 2]
            xl = P.dma("G", sl, h["xT"][t * 128:(t + 1) * 128, :],
                       deps=list(wo_ring_cons[t % 2]), tag=f"woxre{t}")
            ya = P.op("V", lambda v, sl=sl, ps=ps: v.tensor_add(sl, ps, sl),
                      deps=[tok, xl], tag=f"woadd{t}")
            yb = P.op("V", lambda v, t=t, sl=sl: v.tensor_copy(resb_t[t], sl),
                      deps=[ya], tag=f"wocast{t}")
            ysq = P.op("V", lambda v, t=t: v.tensor_mul(sq_t[t], resb_t[t],
                                                        resb_t[t]),
                       deps=[yb], tag=f"wosq{t}")
            sp = P.dma("G", h["x1_hbm"][t], sl, deps=[ya], tag=f"wosp{t}")
            y_sp.append(sp)
            wo_ring_cons[t % 2] = (ya, yb, ysq, sp)
            dump_tile("y1", t, sl, [ya])
            return [ya, yb, ysq]

        proj(wo_evac, h["wo"], NT, attnout, NT, deps=att_done, tag="wo")
        ln1_deps = [P.tok("V")] + y_sp

        def apply_ln(resid_src_hbm, out_bf_tiles, ln_deps, g_cols=None,
                     b_cols=None, out_f32_hbm=None, shifted=False,
                     dump_name=None, tag="ln"):
            stok = ln_stats(stats_src[0], stats_src[1], ln_deps)
            bt1, bt2, BC2 = ln_rows(stok, tag)
            ring = ln_ring
            nr = len(ring)
            prevs = [() for _ in range(nr)]
            toks = []
            for t in range(NT):
                sl = ring[t % nr]
                if shifted and t >= 4:
                    xl = P.dma("G", sl[:, 1:S], resid_src_hbm[t][:, 0:S - 1],
                               deps=[stok] + list(prevs[t % nr]),
                               tag=f"{tag}re{t}")
                    zt = P.op("S", lambda s, sl=sl: s.memzero(sl[:, 0:1]),
                              deps=[xl] + list(prevs[t % nr]), tag=f"{tag}z{t}")
                else:
                    xl = P.dma("G", sl, resid_src_hbm[t],
                               deps=[stok] + list(prevs[t % nr]),
                               tag=f"{tag}re{t}")
                    zt = None
                m1 = P.op("V", lambda v, sl=sl: v.tensor_mul(sl, sl, BC[:, :]),
                          deps=[xl, bt1, zt] + list(prevs[t % nr]),
                          tag=f"{tag}m{t}")
                m2 = P.op("V", lambda v, sl=sl: v.tensor_add(sl, sl, BC2),
                          deps=[m1, bt2], tag=f"{tag}a{t}")
                if g_cols is not None:
                    m3 = P.op("V", lambda v, sl=sl, t=t: v.tensor_scalar(
                        out_bf_tiles[t], sl, g_cols[t], b_cols[t],
                        ALU.mult, ALU.add), deps=[m2], tag=f"{tag}g{t}")
                else:
                    m3 = P.op("V", lambda v, sl=sl, t=t:
                              v.tensor_copy(out_bf_tiles[t], sl),
                              deps=[m2], tag=f"{tag}c{t}")
                last = (m3,)
                if out_f32_hbm is not None:
                    sp = P.dma("G", out_f32_hbm[t], sl, deps=[m2],
                               tag=f"{tag}sp{t}")
                    toks.append(sp)
                    last = (m3, sp)
                if dump_name:
                    dump_tile(dump_name, t, sl, [m2])
                prevs[t % nr] = last
                toks.append(m3)
            return toks

        ln_ring = [A4[:, 2048:4096], A5[:, 2048:4096],
                   A6[:, 0:2048], A6[:, 2048:4096]]
        stats_src = (resb_t, sq_t)
        ln1_toks = apply_ln(h["x1_hbm"], resb_t, ln1_deps,
                            out_f32_hbm=h["x1_hbm"], dump_name="x1", tag="l1")

        if upto == "mla":
            prevs = ()
            for t in range(NT):
                sl = A4[:, 0:2048]
                xl = P.dma("G", sl, h["x1_hbm"][t], deps=ln1_toks + list(prevs),
                           tag=f"dbgre{t}")
                od = P.dma("G", h["outT"][t * 128:(t + 1) * 128, :], sl,
                           deps=[xl], tag=f"dbgout{t}")
                prevs = (od,)
            P.join(engs=("G",))
            P.emit(blk)
            return nc

        def ffn_iter(l, in_hbm, mid_hbm, last_toks, stop_at=None):
            sh_tiles = [A1[:, i * 1024:(i + 1) * 1024].bitcast(BF16)
                        for i in range(4)]
            shtoks = []
            for i in range(4):
                t1 = P.op("V", lambda v, i=i: v.tensor_copy(
                    sh_tiles[i][:, 1:S], resb_t[4 + i][:, 0:S - 1]),
                    deps=last_toks, tag=f"sh{i}")
                t2 = P.op("V", lambda v, i=i: v.memset(sh_tiles[i][:, 0:1], 0.0),
                          deps=[t1], tag=f"shz{i}")
                shtoks.append(t2)
            rhs_h = resb_t[0:4] + sh_tiles

            qtok = P.dma("Y", qs_col, h["fqs"][l], tag="qs")
            ktok = P.dma("Y", ks_col, h["fks"][l], tag="ks")
            cmap = [A3[:, 0:2048], A3[:, 2048:4096]]
            m_tok1 = P.dma("Y", cmap[0], h["rot_qc1"][:], deps=last_toks, tag="c1")
            m_tok2 = P.dma("Y", cmap[1], h["rot_qc2"][:], tag="c2")

            qrot = [A2[:, t * 1024:(t + 1) * 1024].bitcast(BF16) for t in range(4)]
            krot = [A2[:, 4096 + t * 1024: 5120 + t * 1024].bitcast(BF16)
                    for t in range(4)]
            qraw_ring = [A5[:, 0:2048], A5[:, 2048:4096]]
            T32_ring = [A6[:, 0:2048], A6[:, 2048:4096]]
            q2b_ring = [A4[:, 0:1024].bitcast(BF16), A4[:, 1024:2048].bitcast(BF16)]
            Tb_ring = [A4[:, 2048:3072].bitcast(BF16)] * 2
            PTb_ring = [A4[:, 3072:4096].bitcast(BF16)] * 2
            u_sc = [A1[:, 4096:6144], A1[:, 6144:8192]]
            lvl = {"f1a": 1, "f1b": 2, "f1c": 3}.get(stop_at, 9)
            psA_cons = ()
            stats_cons = ()
            tb_cons = ()
            ptb_cons = ()
            ring_cons = [(), ()]
            rot_toks = []
            for t in range(NT):
                is_q = t < 4
                slot = t % 2
                if t == 4:
                    m_tok1 = P.dma("Y", cmap[0], h["rot_kc3"][:],
                                   deps=list(ring_cons[0]) + list(ring_cons[1]),
                                   tag="c3")
                    m_tok2 = P.dma("Y", cmap[1], h["rot_kc4"][:], tag="c4")
                wbuf, wtok = wstream(h["fwqk"][l][t], NT, slot,
                                     deps=shtoks if t < 2 else ())
                dd = [wtok, qtok, ktok, m_tok1, m_tok2] + shtoks + list(psA_cons)
                tok = mm_group(psA, lambda c, wbuf=wbuf: wbuf[:, c * 128:(c + 1) * 128],
                               lambda c, ncx: rhs_h[c][:, ncx * 512:(ncx + 1) * 512],
                               NT, dd, f"fqkv{t}")
                ws_state[slot] = (tok,)
                qraw = qraw_ring[slot]
                e1a = P.op("V", lambda v, qraw=qraw:
                           v.tensor_copy(qraw[:, 0:1024], psA[:, 0:1024]),
                           deps=[tok] + list(ring_cons[slot]), tag=f"fqr{t}")
                e1b = P.op("S", lambda s, qraw=qraw:
                           s.copy(qraw[:, 1024:2048], psA[:, 1024:2048]),
                           deps=[tok] + list(ring_cons[slot]), tag=f"fqs{t}")
                q2b = q2b_ring[slot]
                e2 = P.op("S", lambda s, q2b=q2b, qraw=qraw:
                          s.activation(q2b, qraw, AF.Square),
                          deps=[e1a, e1b], tag=f"fsq{t}")
                psA_cons = (e1a, e1b)
                if lvl < 2:
                    ring_cons[slot] = (e1a, e1b, e2)
                    continue
                sstok = None
                for hh in range(2):
                    for ncx in range(4):
                        def mm(T, hh=hh, ncx=ncx, q2b=q2b):
                            return T.matmul(
                                psB[32 * hh:32 * hh + 1, ncx * 512:(ncx + 1) * 512],
                                ones_bf[hh * 64:hh * 64 + 64, 0:1],
                                q2b[hh * 64:hh * 64 + 64, ncx * 512:(ncx + 1) * 512],
                                start=True, stop=True,
                                tile_position=(hh * 64, hh * 32))
                        sstok = P.op("T", mm,
                                     deps=([e2] + list(stats_cons))
                                     if (hh == 0 and ncx == 0) else (),
                                     inc=(1 if (hh == 1 and ncx == 3) else 0),
                                     tag=f"fss{t}")
                r_toks = []
                for hh, base in ((0, 0), (1, 32)):
                    r1 = P.op("V", lambda v, base=base: v.tensor_scalar_max(
                        rowA[base:base + 1, :], psB[base:base + 1, :], 1e-24),
                        deps=[sstok], tag=f"frn{t}_{hh}")
                    r2 = P.op("S", lambda s, base=base: s.activation(
                        rowA[base:base + 1, :], rowA[base:base + 1, :], AF.Ln),
                        deps=[r1], tag=f"frl{t}_{hh}")
                    r3 = P.op("S", lambda s, base=base: s.activation(
                        rowA[base:base + 1, :], rowA[base:base + 1, :], AF.Exp,
                        scale=-0.5), deps=[r2], tag=f"fre{t}_{hh}")
                    b_ = bcast(rowA[base:base + 1, :], BC[hh * 64:hh * 64 + 64, :],
                               64, 6 + hh, deps=[r3], tag=f"fbc{t}_{hh}")
                    r_toks.append((r1, b_))
                stats_cons = (r_toks[0][0], r_toks[1][0])
                if lvl < 3:
                    ring_cons[slot] = (e1a, e1b, e2)
                    continue
                sc = qs_col if is_q else ks_col
                T32 = T32_ring[slot]
                v1 = P.op("V", lambda v, T32=T32, qraw=qraw, sc=sc:
                          v.scalar_tensor_tensor(T32, BC[:, :], sc[:, 0:1],
                                                 qraw, ALU.mult, ALU.mult),
                          deps=[r_toks[0][1], r_toks[1][1], e1a, e1b],
                          tag=f"fT{t}")
                Tb = Tb_ring[slot]
                v2 = P.op("S", lambda s, Tb=Tb, T32=T32: s.copy(Tb, T32),
                          deps=[v1] + list(tb_cons), tag=f"fTb{t}")
                PTb = PTb_ring[slot]
                rs = []
                for (a, b) in ((0, 32), (32, 0), (64, 96), (96, 64)):
                    rs.append(P.dma("Y", PTb[a:a + 32, :], Tb[b:b + 32, :],
                                    deps=[v2] + list(ptb_cons), tag=f"frs{t}"))
                usc = u_sc[slot]
                v3 = P.op("V", lambda v, usc=usc, PTb=PTb:
                          v.tensor_mul(usc, PTb, cmap[1]), deps=rs, tag=f"fu{t}")
                tb_cons = tuple(rs)
                ptb_cons = (v3,)
                v4 = P.op("G", lambda g, T32=T32: g.tensor_mul(T32, T32, cmap[0]),
                          deps=[v1, v2, m_tok1], tag=f"fm{t}")
                dst = (qrot if is_q else krot)[t % 4]
                v5 = P.op("V", lambda v, dst=dst, usc=usc, T32=T32:
                          v.tensor_add(dst, T32, usc), deps=[v3, v4],
                          tag=f"fr{t}")
                ring_cons[slot] = (v5, v2)
                rot_toks.append(v5)
            rot_done = [P.tok("V"), P.tok("S")]
            if lvl < 9:
                return rot_done + [P.tok("T")] + P.dtok("G") + P.dtok("Y")

            vrow = A6[:, 0:4096].bitcast(BF16)
            fwv_stage = A4[:, 0:2048]
            fwv_buf = A4[:, 2048:4096].bitcast(BF16)
            fl = h["fwv"][l].rearrange("c p f -> p c f")
            st3 = fwv_stage.rearrange("p (c f) -> p c f", f=512)
            lv1 = P.dma("Y", st3, fl[:, 0:4, :], deps=rot_done, tag="fwv0")
            lc1 = P.op("V", lambda v: v.tensor_copy(fwv_buf[:, 0:2048], fwv_stage),
                       deps=[lv1], tag="fwvc0")
            lv2 = P.dma("Y", st3, fl[:, 4:8, :], deps=[lc1], tag="fwv1")
            lc2 = P.op("V", lambda v: v.tensor_copy(fwv_buf[:, 2048:4096],
                                                    fwv_stage),
                       deps=[lv2], tag="fwvc1")
            vrd = [(), ()]
            for tt in range(16):
                slot = tt % 2
                ps = (psA if slot == 0 else psB)[:, 0:512]
                dd = [lc1, lc2] + list(vrd[slot]) + rot_done
                tok = None
                for c in range(NT):
                    def mm(T, tt=tt, c=c, ps=ps):
                        return T.matmul(ps, rhs_h[c][:, tt * 128:(tt + 1) * 128],
                                        fwv_buf[:, c * 512:(c + 1) * 512],
                                        start=(c == 0), stop=(c == NT - 1))
                    tok = P.op("T", mm, deps=dd if c == 0 else (),
                               inc=(1 if c == NT - 1 else 0), tag=f"fv{tt}")
                e = P.op("V", lambda v, tt=tt, ps=ps:
                         v.tensor_copy(vrow[:, tt * 512:(tt + 1) * 512], ps),
                         deps=[tok], tag=f"fvv{tt}")
                vrd[slot] = (e,)
            v_done = [P.tok("V")]
            if stop_at == "f1":
                return v_done + rot_done

            mtok = P.dma("Y", A3[:, 0:3200].rearrange("p (m f) -> p m f", f=640),
                         h["lmask"].rearrange("m p f -> p m f"),
                         deps=v_done, tag="lmask")
            masks = [A3[:, i * 640:(i + 1) * 640] for i in range(5)]
            o_row = A5[:, 0:4096].bitcast(BF16)
            smask_r = [A4[:, 0:640], A4[:, 640:1280]]
            ae_r = [A4[:, 1280 + i * 320: 1600 + i * 320].bitcast(BF16)
                    for i in range(2)]
            aeT_r = [A4[:, 1920 + i * 320: 2240 + i * 320].bitcast(BF16)
                     for i in range(2)]
            simP = [psA[:, 0:640], psA[:, 1024:1664]]
            outP = [psA[:, 768:832], psA[:, 1792:1856]]
            aeTP = [psB[:, 0:640], psB[:, 1024:1664]]
            grp_cons = [v_done, v_done]
            aeT_cons = [(), ()]
            sm_war = {}
            ae_war = {}
            rc_war = {}
            pend2 = []
            for ui in range(128):
                qt, hh = ui // 8, ui % 8
                g = ui % 2
                ftile = hh // 2
                hb = (hh % 2) * 64
                kw = max(0, qt * 128 - 512)
                mask = masks[qt] if qt < 4 else masks[4]
                dd = [mtok] + list(grp_cons[g])
                stok = None
                for part in range(2):
                    lo, hi = part * 512, min(640, (part + 1) * 512)
                    def mm(T, ftile=ftile, hb=hb, qt=qt, kw=kw, lo=lo, hi=hi, g=g):
                        return T.matmul(
                            simP[g][:, lo:hi],
                            qrot[ftile][hb:hb + 64, qt * 128:(qt + 1) * 128],
                            krot[ftile][hb:hb + 64, kw + lo:kw + hi],
                            start=True, stop=True, tile_position=(hb, 0))
                    stok = P.op("T", mm, deps=dd if part == 0 else (),
                                inc=(part == 1), tag=f"si{ui}")
                sm = smask_r[g]
                m1 = P.op("V", lambda v, g=g, sm=sm, mask=mask:
                          v.tensor_add(sm, simP[g], mask),
                          deps=[stok, sm_war.get(g)], tag=f"lm{ui}")
                nm = nm_cols[g]
                m2 = P.op("V", lambda v, sm=sm, nm=nm:
                          v.reduce_max(nm[:, 0:1], sm, axis=AXX, negate=True),
                          deps=[m1], tag=f"lx{ui}")
                while len(pend2) >= 2:
                    pend2.pop(0)()
                ae = ae_r[g]
                rc = rc_cols[g]
                m3 = P.op("S", lambda s, sm=sm, ae=ae, nm=nm, rc=rc:
                          s.activation(ae, sm, AF.Exp, bias=nm[:, 0:1],
                                       accum_out=rc[:, 0:1]),
                          deps=[m2, ae_war.get(g), rc_war.get(g)],
                          tag=f"le{ui}")
                sm_war[g] = m3
                grp_cons[g] = (m1,)

                def tail(ui=ui, qt=qt, hh=hh, g=g, kw=kw, ae=ae, rc=rc, m3=m3):
                    dd2 = [m3] + list(aeT_cons[g])
                    ttok = None
                    for b in range(5):
                        def mm(T, ae=ae, b=b, g=g):
                            return T.matmul(aeTP[g][:, b * 128:(b + 1) * 128],
                                            ae[:, b * 128:(b + 1) * 128],
                                            ident_bf[:, 0:128],
                                            start=True, stop=True)
                        ttok = P.op("T", mm, deps=dd2 if b == 0 else (),
                                    inc=(b == 4), tag=f"lt{ui}")
                    ae_war[g] = ttok
                    aeT = aeT_r[g]
                    m4 = P.op("S", lambda s, g=g, aeT=aeT: s.copy(aeT, aeTP[g]),
                              deps=[ttok], tag=f"la{ui}")
                    aeT_cons[g] = (m4,)
                    otok = None
                    for b in range(5):
                        tv = kw // 128 + b
                        def mm(T, aeT=aeT, b=b, g=g, tv=tv, hh=hh):
                            return T.matmul(
                                outP[g], aeT[:, b * 128:(b + 1) * 128],
                                vrow[:, tv * 512 + hh * 64:
                                     tv * 512 + hh * 64 + 64],
                                start=(b == 0), stop=(b == 4))
                        otok = P.op("T", mm, deps=[m4] if b == 0 else (),
                                    inc=(b == 4), tag=f"lo{ui}")
                    m5 = P.op("V", lambda v, rc=rc: v.reciprocal(rc[:, 0:1],
                                                                 rc[:, 0:1]),
                              deps=[m3], tag=f"lr{ui}")
                    m6 = P.op("V", lambda v, g=g, qt=qt, hh=hh, rc=rc:
                              v.tensor_scalar_mul(
                                  o_row[:, qt * 512 + hh * 64:
                                        qt * 512 + hh * 64 + 64],
                                  outP[g], rc[:, 0:1]),
                              deps=[otok, m5], tag=f"ln{ui}")
                    rc_war[g] = m6
                    grp_cons[g] = grp_cons[g] + (m6,)
                pend2.append(tail)
            while pend2:
                pend2.pop(0)()
            lat_done = [P.tok("V"), P.tok("S")]
            if stop_at == "f2":
                return lat_done

            oT = [A2[:, t * 1024:(t + 1) * 1024].bitcast(BF16) for t in range(4)]
            otr_cons = [lat_done, lat_done]
            for ft in range(4):
                for tt in range(16):
                    g = tt % 2
                    ps = (psA if g == 0 else psB)[:, 0:128]
                    def mm(T, ft=ft, tt=tt, ps=ps):
                        return T.matmul(ps, o_row[:, tt * 512 + ft * 128:
                                                  tt * 512 + (ft + 1) * 128],
                                        ident_bf[:, 0:128], start=True, stop=True)
                    tok = P.op("T", mm, deps=list(otr_cons[g]) + lat_done,
                               inc=1, tag=f"ot{ft}_{tt}")
                    e = P.op("V", lambda v, ft=ft, tt=tt, ps=ps:
                             v.tensor_copy(oT[ft][:, tt * 128:(tt + 1) * 128], ps),
                             deps=[tok], tag=f"ote{ft}_{tt}")
                    otr_cons[g] = (e,)
            oT_done = [P.tok("V")]

            sq2 = [A1[:, t * 1024:(t + 1) * 1024].bitcast(BF16) for t in range(NT)]
            y_ring2 = [A5[:, 0:2048], A5[:, 2048:4096]]
            sp2 = []
            fwo_ring_cons = [tuple(oT_done), tuple(oT_done)]

            def fwo_evac(t, ps, tok):
                sl = y_ring2[t % 2]
                xl = P.dma("G", sl, in_hbm[t],
                           deps=list(fwo_ring_cons[t % 2]), tag=f"fwox{t}")
                ya = P.op("V", lambda v, sl=sl, ps=ps: v.tensor_add(sl, ps, sl),
                          deps=[tok, xl], tag=f"fwoa{t}")
                yb = P.op("V", lambda v, t=t, sl=sl: v.tensor_copy(resb_t[t], sl),
                          deps=[ya], tag=f"fwoc{t}")
                ysq = P.op("V", lambda v, t=t: v.tensor_mul(sq2[t], resb_t[t],
                                                            resb_t[t]),
                           deps=[yb], tag=f"fwos{t}")
                sp = P.dma("G", mid_hbm[t], sl, deps=[ya], tag=f"fwosp{t}")
                sp2.append(sp)
                fwo_ring_cons[t % 2] = (ya, yb, ysq, sp)
                dump_tile("x2_%d" % l, t, sl, [ya])
                return [ya, yb, ysq]

            if "fwo" in joins:
                P.join()
            proj(fwo_evac, h["fwo"][l], 4, oT, NT, deps=oT_done, tag="fwo")
            x2_done = [P.tok("V")] + sp2
            if stop_at == "f3":
                return x2_done

            s2b = [A6[:, i * 1024:(i + 1) * 1024].bitcast(BF16) for i in range(4)]
            s2toks = []
            for i in range(4):
                t1 = P.op("V", lambda v, i=i: v.tensor_copy(
                    s2b[i][:, 1:S], resb_t[4 + i][:, 0:S - 1]),
                    deps=x2_done, tag=f"s2{i}")
                t2 = P.op("V", lambda v, i=i: v.memset(s2b[i][:, 0:1], 0.0),
                          deps=[t1], tag=f"s2z{i}")
                s2toks.append(t2)
            h2 = resb_t[0:4] + s2b
            sq2toks = []
            for t in range(4, NT):
                tk = P.op("V", lambda v, t=t: v.tensor_mul(sq2[t], h2[t], h2[t]),
                          deps=[s2toks[t - 4]], tag=f"s2q{t}")
                sq2toks.append(tk)
            gtok = P.dma("Y", lng_cols, h["flng"][l], tag="lng")
            btok = P.dma("Y", lnb_cols, h["flnb"][l], tag="lnb")
            g_cols = [lng_cols[:, t:t + 1] for t in range(NT)]
            b_cols = [lnb_cols[:, t:t + 1] for t in range(NT)]
            lnb_tiles = [A2[:, t * 1024:(t + 1) * 1024].bitcast(BF16)
                         for t in range(NT)]
            globals()["stats_src"] = (h2, sq2)
            stats_src2 = (h2, sq2)
            if "ln2" in joins:
                P.join()
            stok = ln_stats(h2, sq2, x2_done + s2toks + sq2toks + [gtok, btok])
            bt1, bt2, BC2 = ln_rows(stok, f"l2_{l}")
            prevs = [(), ()]
            ln2_toks = []
            ring2x = [A4[:, 0:2048], A6[:, 0:2048],
                      A5[:, 2048:4096], A4[:, 2048:4096]]
            nr2 = 4
            prevs = [() for _ in range(nr2)]
            for t in range(NT):
                sl = ring2x[t % nr2]
                if t >= 4:
                    xl = P.dma("G", sl[:, 1:S], mid_hbm[t][:, 0:S - 1],
                               deps=[stok] + list(prevs[t % nr2]),
                               tag=f"l2re{t}")
                    zt = P.op("S", lambda s, sl=sl: s.memzero(sl[:, 0:1]),
                              deps=[xl] + list(prevs[t % nr2]), tag=f"l2z{t}")
                else:
                    xl = P.dma("G", sl, mid_hbm[t],
                               deps=[stok] + list(prevs[t % nr2]),
                               tag=f"l2re{t}")
                    zt = None
                m1 = P.op("V", lambda v, sl=sl: v.tensor_mul(sl, sl, BC[:, :]),
                          deps=[xl, bt1, zt] + list(prevs[t % nr2]),
                          tag=f"l2m{t}")
                m2 = P.op("V", lambda v, sl=sl: v.tensor_add(sl, sl, BC2),
                          deps=[m1, bt2], tag=f"l2a{t}")
                dump_tile("ln2pre", t, sl, [m2])
                m3 = P.op("V", lambda v, sl=sl, t=t: v.tensor_scalar(
                    lnb_tiles[t], sl, g_cols[t], b_cols[t], ALU.mult, ALU.add),
                    deps=[m2], tag=f"l2g{t}")
                prevs[t % nr2] = (m3,)
                ln2_toks.append(m3)
            if stop_at == "ln2":
                prevs2 = ()
                for t in range(NT):
                    sl = A4[:, 0:2048]
                    cv = P.op("V", lambda v, t=t, sl=sl:
                              v.tensor_copy(sl, lnb_tiles[t]),
                              deps=ln2_toks + list(prevs2), tag=f"lnd{t}")
                    od = P.dma("G", h["outT"][t * 128:(t + 1) * 128, :], sl,
                               deps=[cv], tag=f"lno{t}")
                    prevs2 = (od,)
                return P.dtok("G")

            if "lnbd" in dumps:
                dprev = ()
                for t in range(NT):
                    sl = BC[:, 0:2048]
                    cv = P.op("V", lambda v, t=t, sl=sl:
                              v.tensor_copy(sl, lnb_tiles[t]),
                              deps=ln2_toks + list(dprev), tag=f"lnbd{t}")
                    od = P.dma("G", dumps["lnbd"][t * 128:(t + 1) * 128, :], sl,
                               deps=[cv], tag=f"lnbo{t}")
                    dprev = (od,)

            if "ln2w1" in joins:
                P.join()
            b1t = [P.dma("Y", b1a_cols, h["fb1a"][l], tag="b1a"),
                   P.dma("Y", b1g_cols, h["fb1g"][l], tag="b1g"),
                   P.dma("Y", b2_cols, h["fb2"][l], tag="b2")]
            b1s = P.op("V", lambda v: v.tensor_scalar_mul(b1gs_cols, b1g_cols,
                                                          SQ2I),
                       deps=[b1t[1]], tag="b1gs")
            gated = [A1[:, j * 512:(j + 1) * 512].bitcast(BF16)
                     for j in range(16)] + \
                    [A3[:, j * 512:(j + 1) * 512].bitcast(BF16) for j in range(6)]
            a_r = [A4[:, 2048 + i * 1024: 3072 + i * 1024] for i in range(2)]
            e_r = [A5[:, i * 1024:(i + 1) * 1024] for i in range(2)]
            g_r = [A5[:, 2048 + i * 1024: 3072 + i * 1024] for i in range(2)]
            y2_r = [A4[:, 0:2048], A6[:, 2048:4096]]
            w2tmp = A6[:, 0:2048]
            w2_out_toks = []
            resid_by_tile = [[] for _ in range(NT)]
            gate_cons = [(), ()]
            w2tmp_cons = ()
            w2_cons = [tuple(prevs[0]), tuple(prevs[1])]
            psA_w1 = ()
            psB_w1 = ()
            resid_toks = []
            for half in range(2):
                off = half * 1024
                half_gtoks = []
                for j in range(NTH):
                    slot = j % 2
                    ar, er, gr = a_r[slot], e_r[slot], g_r[slot]
                    wbuf, wtok = wstream(h["fw1a"][l][j], NT, 0,
                                         deps=ln2_toks if (half == 0 and j < 2)
                                         else ())
                    dd = [wtok, b1t[0], b1s] + ln2_toks + \
                        list(gate_cons[slot]) + list(psA_w1) + \
                        list(w2_cons[0]) + list(w2_cons[1])
                    tok = mm_group(
                        psA[:, 0:1024],
                        lambda c, wbuf=wbuf: wbuf[:, c * 128:(c + 1) * 128],
                        lambda c, ncx, off=off: lnb_tiles[c][:, off + ncx * 512:
                                                             off + (ncx + 1) * 512],
                        NT, dd, f"w1a{half}_{j}", n_nc=2)
                    ws_state[0] = (tok,)
                    ea = P.op("S", lambda s, ar=ar, j=j: s.activation(
                        ar[:, 0:1024], psA[:, 0:1024], AF.Identity,
                        bias=b1a_cols[:, j:j + 1]),
                        deps=[tok], tag=f"w1ae{half}_{j}")
                    psA_w1 = (ea,)
                    wbuf2, wtok2 = wstream(h["fw1g"][l][j], NT, 1, deps=())
                    dd2 = [wtok2] + list(psB_w1) + \
                        list(w2_cons[0]) + list(w2_cons[1])
                    tok2 = mm_group(
                        psB[:, 0:1024],
                        lambda c, wbuf2=wbuf2: wbuf2[:, c * 128:(c + 1) * 128],
                        lambda c, ncx, off=off: lnb_tiles[c][:, off + ncx * 512:
                                                             off + (ncx + 1) * 512],
                        NT, dd2, f"w1g{half}_{j}", n_nc=2)
                    ws_state[1] = (tok2,)
                    ee = P.op("S", lambda s, er=er, j=j: s.activation(
                        er[:, 0:1024], psB[:, 0:1024], AF.Gelu,
                        bias=b1g_cols[:, j:j + 1]),
                        deps=[tok2], tag=f"w1gl{half}_{j}")
                    psB_w1 = (ee,)
                    g2 = P.op("V", lambda v, ar=ar, er=er, j=j:
                              v.tensor_mul(gated[j], ar[:, 0:1024], er[:, 0:1024]),
                              deps=[ee] + list(w2_cons[0]) + list(w2_cons[1]),
                              tag=f"w1o{half}_{j}")
                    if half == 0 and j == 0 and "w1dbg" in dumps:
                        dbg = dumps["w1dbg"]
                        d1 = P.dma("G", dbg[0:128, 0:1024], ar[:, 0:1024],
                                   deps=[g2], tag="dbg_a")
                        d2 = P.dma("G", dbg[128:256, 0:1024], gr[:, 0:1024],
                                   deps=[g2], tag="dbg_g")
                        d3 = P.dma("G", dbg[256:384, 0:1024], er[:, 0:1024],
                                   deps=[g2], tag="dbg_e")
                        cv = P.op("V", lambda v: v.tensor_copy(
                            BC[:, 0:1024], gated[0][:, 0:1024]),
                            deps=[g2], tag="dbg_gc")
                        d4 = P.dma("G", dbg[384:512, 0:1024], BC[:, 0:1024],
                                   deps=[cv], tag="dbg_gt")
                        gate_cons[slot] = (g2, ea, d1, d2, d3, d4)
                    else:
                        gate_cons[slot] = (g2, ea)
                    half_gtoks.append(g2)
                if "w1w2" in joins:
                    P.join()
                for e in ("Y", "G", "T", "V", "S"):
                    if ("w1w2" + e) in joins:
                        P.join(engs=(e,))
                for t in range(NT):
                    slot = t % 2
                    ps = psA if slot == 0 else psB
                    tok = None
                    for grp in range(3):
                        c0, c1 = grp * 8, min(NTH, grp * 8 + 8)
                        wbuf, wtok = wstream(h["fw2"][l][t][c0:c1], c1 - c0,
                                             (t + grp) % 2,
                                             deps=half_gtoks[-1:])
                        dd = [wtok] + half_gtoks + list(w2_cons[slot]) + b1t
                        tok = mm_group(
                            ps[:, 0:1024],
                            lambda ci, wbuf=wbuf: wbuf[:, ci * 128:(ci + 1) * 128],
                            lambda ci, ncx, c0=c0: gated[c0 + ci][:, ncx * 512:
                                                                  (ncx + 1) * 512],
                            c1 - c0, dd, f"w2_{half}_{t}_{grp}", n_nc=2,
                            start_c=(0 if grp == 0 else -1),
                            stop_c=(c1 - c0 - 1 if grp == 2 else -2))
                        ws_state[slot] = (tok,)
                    sl = y2_r[t % 2]
                    xl = P.dma("G", sl[:, 0:1024], mid_hbm[t][:, off:off + 1024],
                               deps=list(w2_cons[t % 2]), tag=f"w2x{half}_{t}")
                    ya = P.op("V", lambda v, ps=ps, sl=sl: v.scalar_tensor_tensor(
                        sl[:, 0:1024], ps[:, 0:1024], 1.0, sl[:, 0:1024],
                        ALU.mult, ALU.add),
                        deps=[tok, xl] + list(w2_cons[t % 2]),
                        tag=f"w2b{half}_{t}")
                    yb = P.op("V", lambda v, sl=sl, t=t: v.tensor_scalar_add(
                        sl[:, 0:1024], sl[:, 0:1024], b2_cols[:, t:t + 1]),
                        deps=[ya], tag=f"w2a{half}_{t}")
                    yc = P.op("V", lambda v, t=t, sl=sl, off=off: v.tensor_copy(
                        resb_t[t][:, off:off + 1024], sl[:, 0:1024]),
                        deps=[yb], tag=f"w2c{half}_{t}")
                    sp = P.dma("G", mid_hbm[t][:, off:off + 1024], sl[:, 0:1024],
                               deps=[yb], tag=f"w2sp{half}_{t}")
                    resid_toks.append(sp)
                    resid_by_tile[t].append(sp)
                    w2_cons[t % 2] = (yb, yc, ya, sp)
            if "lnbd2" in dumps:
                dprev = ()
                cur = [P.tok("V"), P.tok("S"), P.tok("T")] + P.dtok("G")
                for t in range(NT):
                    sl = BC[:, 0:2048]
                    cv = P.op("V", lambda v, t=t, sl=sl:
                              v.tensor_copy(sl, lnb_tiles[t]),
                              deps=cur + list(dprev), tag=f"ln2d{t}")
                    od = P.dma("G", dumps["lnbd2"][t * 128:(t + 1) * 128, :], sl,
                               deps=[cv], tag=f"ln2o{t}")
                    dprev = (od,)
            it_ret = [P.tok("V"), P.tok("S")] + resid_toks
            it_ret_by_tile[:] = [list(x) for x in resid_by_tile]
            return it_ret

        if upto in ("f1", "f2", "f3", "f1a", "f1b", "f1c", "ln2"):
            f3t = ffn_iter(0, h["x1_hbm"], h["res_hbm"], ln1_toks, stop_at=upto)
            prevs = ()
            for t in (range(NT) if upto != "ln2" else []):
                sl = A4[:, 0:2048]
                xl = P.dma("G", sl, h["res_hbm"][t], deps=f3t + list(prevs),
                           tag=f"dre{t}")
                od = P.dma("G", h["outT"][t * 128:(t + 1) * 128, :], sl,
                           deps=[xl], tag=f"dout{t}")
                prevs = (od,)
            P.join(engs=("G",))
            P.emit(blk)
            return nc
        it_ret_by_tile = [[] for _ in range(NT)]
        it1 = ffn_iter(0, h["x1_hbm"], h["res_hbm"], ln1_toks)
        if upto == "it1":
            prevs = ()
            for t in range(NT):
                sl = A4[:, 0:2048]
                xl = P.dma("G", sl, h["res_hbm"][t], deps=it1 + list(prevs),
                           tag=f"dre{t}")
                od = P.dma("G", h["outT"][t * 128:(t + 1) * 128, :], sl,
                           deps=[xl], tag=f"dout{t}")
                prevs = (od,)
            P.join(engs=("G",))
            P.emit(blk)
            return nc

        it2 = ffn_iter(1, h["res_hbm"], h["res_hbm"], it1)

        zb = [A2[:, t * 1024:(t + 1) * 1024].bitcast(BF16) for t in range(NT)]
        zsq = [A1[:, t * 1024:(t + 1) * 1024].bitcast(BF16) for t in range(NT)]
        z_ring = [A4[:, 0:2048], A4[:, 2048:4096]]
        z2_ring = [A6[:, 0:2048], A6[:, 2048:4096]]
        ztoks = []
        zsp = []
        prevz = [(), ()]
        for t in range(NT):
            sl = z_ring[t % 2]
            sl2 = z2_ring[t % 2]
            x1l = P.dma("G", sl, h["x1_hbm"][t],
                        deps=(it2 if t < 2 else []) + list(prevz[t % 2]),
                        tag=f"zx{t}")
            fl_ = P.dma("G", sl2, h["res_hbm"][t],
                        deps=it_ret_by_tile[t], tag=f"zf{t}")
            za = P.op("V", lambda v, sl=sl, sl2=sl2: v.tensor_add(sl, sl, sl2),
                      deps=[x1l, fl_], tag=f"za{t}")
            zc = P.op("V", lambda v, t=t, sl=sl: v.tensor_copy(zb[t], sl),
                      deps=[za], tag=f"zc{t}")
            zq = P.op("V", lambda v, t=t: v.tensor_mul(zsq[t], zb[t], zb[t]),
                      deps=[zc], tag=f"zq{t}")
            sp = P.dma("G", h["x1_hbm"][t], sl, deps=[za], tag=f"zs{t}")
            zsp.append(sp)
            prevz[t % 2] = (za, zc, zq, sp)
            ztoks.extend([zq])
        stok = ln_stats(zb, zsq, ztoks)
        bt1, bt2, BC2 = ln_rows(stok, "f")
        prevs = [tuple(prevz[0]), tuple(prevz[1])]
        for t in range(NT):
            sl = z_ring[t % 2]
            xl = P.dma("G", sl, h["x1_hbm"][t],
                       deps=[bt1, zsp[t]] + list(prevs[t % 2]), tag=f"for{t}")
            m1 = P.op("V", lambda v, sl=sl: v.tensor_mul(sl, sl, BC[:, :]),
                      deps=[xl, bt1], tag=f"fom{t}")
            m2 = P.op("V", lambda v, sl=sl: v.tensor_add(sl, sl, BC2),
                      deps=[m1, bt2], tag=f"foa{t}")
            od = P.dma("G", h["outT"][t * 128:(t + 1) * 128, :], sl,
                       deps=[m2], tag=f"foo{t}")
            prevs[t % 2] = (od,)
        P.join(engs=("G",))
        P.emit(blk)
    return nc



def kernel(**inputs):
    sh, xts = _host_prep(inputs)
    nc = build()
    in_maps = []
    for c in range(8):
        m = dict(sh)
        m["xT"] = xts[c]
        in_maps.append(m)
    res = run_bass_kernel_spmd(nc, in_maps, core_ids=list(range(8)))
    outs = [np.asarray(r["outT"]).T for r in res.results]
    return np.ascontiguousarray(np.stack(outs)).astype(np.float32)
